# version 01
# baseline (speedup 1.0000x reference)
import numpy as np
import concourse.bass as bass
import concourse.mybir as mybir
from concourse.bass_utils import run_bass_kernel_spmd

F32 = mybir.dt.float32
BF16 = mybir.dt.bfloat16
I32 = mybir.dt.int32
AF = mybir.ActivationFunctionType
ALU = mybir.AluOpType
AX = mybir.AxisListType

ENGINES = ["sync", "scalar", "vector", "tensor", "gpsimd"]

B, S, E, H, D = 2, 8192, 1024, 16, 64
NT = 2048
TC = 512
NCH = NT // TC
KT8 = E // 128
NKC = S // TC
NCORES = 8


class Prog:
    def __init__(self):
        self.nc = bass.Bass("TRN2", target_bir_lowering=False)
        self.tape = []

    def din(self, name, shape, dtype=F32):
        return self.nc.dram_tensor(name, list(shape), dtype, kind="ExternalInput").ap()

    def dout(self, name, shape, dtype=F32):
        return self.nc.dram_tensor(name, list(shape), dtype, kind="ExternalOutput").ap()

    def sb(self, name, shape, dtype=F32):
        return self.nc.alloc_sbuf_tensor(name, list(shape), dtype)

    def ps(self, name, shape, dtype=F32):
        return self.nc.alloc_psum_tensor(name, list(shape), dtype)

    def op(self, engine, fn, inc=1):
        self.tape.append((engine, fn, inc))

    def dma(self, out, in_, q="sync", **kw):
        self.op(q, lambda e: e.dma_start(out=out, in_=in_, **kw), inc=16)

    def mm(self, out, lhsT, rhs, start=True, stop=True, **kw):
        self.op("tensor", lambda e: e.matmul(out, lhsT, rhs, start=start, stop=stop, **kw))

    def trans(self, out, in_, ident):
        self.op("tensor", lambda e: e.transpose(out, in_, ident))

    def act(self, out, in_, func, bias=0.0, scale=1.0):
        self.op("scalar", lambda e: e.activation(out, in_, func, bias=bias, scale=scale))

    def scopy(self, out, in_):
        self.op("scalar", lambda e: e.copy(out, in_))

    def vcopy(self, out, in_):
        self.op("vector", lambda e: e.tensor_copy(out, in_))

    def gcopy(self, out, in_):
        self.op("gpsimd", lambda e: e.tensor_copy(out, in_))

    def tt(self, out, a, b, alu):
        self.op("vector", lambda e: e.tensor_tensor(out, a, b, alu))

    def ts(self, out, in0, s1, s2=None, op0=ALU.mult, op1=ALU.bypass):
        self.op("vector", lambda e: e.tensor_scalar(out, in0, s1, s2, op0, op1))

    def stt(self, out, in0, scalar, in1, op0, op1):
        self.op("vector", lambda e: e.scalar_tensor_tensor(out, in0, scalar, in1, op0, op1))

    def red(self, out, in_, alu, axis=AX.X, absval=None):
        self.op("vector", lambda e: e.tensor_reduce(out, in_, axis, alu,
                                                    apply_absolute_value=absval))

    def recip(self, out, in_):
        self.op("vector", lambda e: e.reciprocal(out, in_))

    def memset(self, ap, v, eng="vector"):
        self.op(eng, lambda e: e.memset(ap, v))

    def build(self):
        nc = self.nc
        sem = nc.alloc_semaphore("gsem")
        cums = []
        c = 0
        for (_, _, inc) in self.tape:
            cums.append(c)
            c += inc
        total = c
        used = [en for en in ENGINES if any(t[0] == en for t in self.tape)]
        with nc.Block() as block:
            for en in used:
                entries = [(i, fn, inc) for i, (e, fn, inc) in enumerate(self.tape) if e == en]

                def body(eng, entries=entries):
                    for i, fn, inc in entries:
                        if cums[i] > 0:
                            eng.wait_ge(sem, cums[i])
                        fn(eng).then_inc(sem, inc)
                    eng.wait_ge(sem, total)

                getattr(block, en)(body)
        return nc


class Ctx:

    def __init__(self, P, n_ps=4, stage_w=4096):
        self.P = P
        self.stage_w = stage_w
        self.stage = P.sb("stage", [128, stage_w])
        self.ones = P.sb("ones_c", [128, 128])
        P.memset(self.ones[:], 1.0)
        self.psb = [P.ps(f"psb{i}", [128, TC]) for i in range(n_ps)]
        self.psr = [P.ps(f"psr{i}", [1, TC]) for i in range(2)]
        self.rows = [P.sb(f"rowb{i}", [1, TC]) for i in range(3)]
        self.bcs = [P.sb(f"bcb{i}", [128, TC]) for i in range(2)]
        self.tmps = [P.sb(f"tmpb{i}", [128, TC]) for i in range(2)]
        self.cols = [P.sb(f"colb{i}", [128, 1]) for i in range(3)]


def wfill(P, ctx, wsb, hbm, K, N, scale=None):
    stage_w = ctx.stage_w
    for kt in range(K // 128):
        for c0 in range(0, N, stage_w):
            cw = min(stage_w, N - c0)
            P.dma(ctx.stage[:, 0:cw], hbm[kt * 128:(kt + 1) * 128, c0:c0 + cw])
            dst = wsb[:, kt * N + c0: kt * N + c0 + cw]
            if scale is None:
                P.vcopy(dst, ctx.stage[:, 0:cw])
            else:
                P.ts(dst, ctx.stage[:, 0:cw], scale, None, ALU.mult)


def wload(P, ctx, name, hbm, K, N, scale=None):
    wsb = P.sb(name, [128, (K // 128) * N], BF16)
    wfill(P, ctx, wsb, hbm, K, N, scale)
    return wsb


def aload(P, ctx, dst, dcol0, hbm, c0, cw, width):
    for ft in range(KT8):
        P.dma(ctx.stage[:, 0:cw], hbm[ft * 128:(ft + 1) * 128, c0:c0 + cw])
        P.vcopy(dst[:, ft * width + dcol0: ft * width + dcol0 + cw], ctx.stage[:, 0:cw])


def mm_chain(P, ctx, xsb, xw, wsb, K, N, tc0, tcw, evac, psi=0):
    KT = K // 128
    ps = ctx.psb[psi]
    for nt in range(N // 128):
        for kt in range(KT):
            P.mm(ps[:, 0:tcw], wsb[:, kt * N + nt * 128: kt * N + nt * 128 + 128],
                 xsb[:, kt * xw + tc0: kt * xw + tc0 + tcw],
                 start=(kt == 0), stop=(kt == KT - 1))
        evac(nt, ps[:, 0:tcw])


def row_bcast(P, ctx, row_ap, tcw, nrows=128, bci=0, pbase=0):
    ps = ctx.psb[3]
    P.mm(ps[pbase:pbase + nrows, 0:tcw], ctx.ones[0:1, 0:nrows], row_ap)
    bc = ctx.bcs[bci]
    P.vcopy(bc[pbase:pbase + nrows, 0:tcw], ps[pbase:pbase + nrows, 0:tcw])
    return bc[pbase:pbase + nrows, 0:tcw]


def psum_rows(P, ctx, src_tiles, tcw, pri=0, square=False, nparts=128):
    pr = ctx.psr[pri]
    n = len(src_tiles)
    for i, t in enumerate(src_tiles):
        rhs = t
        if square:
            P.act(ctx.tmps[1][0:nparts, 0:tcw], t, AF.Square)
            rhs = ctx.tmps[1][0:nparts, 0:tcw]
        P.mm(pr[:, 0:tcw], ctx.ones[0:nparts, 0:1], rhs, start=(i == 0), stop=(i == n - 1))
    return pr


def ln_chunk(P, ctx, ysb, yw, tc0, tcw, eps, out_cb, gcol=None, bcol=None):
    tiles = [ysb[:, ft * yw + tc0: ft * yw + tc0 + tcw] for ft in range(KT8)]
    prm = psum_rows(P, ctx, tiles, tcw, pri=0)
    prv = psum_rows(P, ctx, tiles, tcw, pri=1, square=True)
    mean = ctx.rows[0]
    P.ts(mean[:, 0:tcw], prm[:, 0:tcw], 1.0 / 1024.0, None, ALU.mult)
    var = ctx.rows[1]
    P.ts(var[:, 0:tcw], prv[:, 0:tcw], 1.0 / 1024.0, None, ALU.mult)
    m2 = ctx.rows[2]
    P.tt(m2[:, 0:tcw], mean[:, 0:tcw], mean[:, 0:tcw], ALU.mult)
    P.tt(var[:, 0:tcw], var[:, 0:tcw], m2[:, 0:tcw], ALU.subtract)
    P.ts(var[:, 0:tcw], var[:, 0:tcw], eps, None, ALU.add)
    P.act(var[:, 0:tcw], var[:, 0:tcw], AF.Sqrt)
    P.recip(var[:, 0:tcw], var[:, 0:tcw])
    bcm = row_bcast(P, ctx, mean[:, 0:tcw], tcw, bci=0)
    bcr = row_bcast(P, ctx, var[:, 0:tcw], tcw, bci=1)
    for ft in range(KT8):
        t = tiles[ft]
        P.tt(t, t, bcm, ALU.subtract)
        P.tt(t, t, bcr, ALU.mult)
        if gcol is not None:
            P.ts(t, t, gcol[:, ft:ft + 1], bcol[:, ft:ft + 1], ALU.mult, ALU.add)
        out_cb(ft, t)


def rms_chunk(P, ctx, ysb, yw, tc0, tcw, eps, out_cb, gcol=None):
    tiles = [ysb[:, ft * yw + tc0: ft * yw + tc0 + tcw] for ft in range(KT8)]
    prv = psum_rows(P, ctx, tiles, tcw, pri=1, square=True)
    var = ctx.rows[1]
    P.ts(var[:, 0:tcw], prv[:, 0:tcw], 1.0 / 1024.0, eps, ALU.mult, ALU.add)
    P.act(var[:, 0:tcw], var[:, 0:tcw], AF.Sqrt)
    P.recip(var[:, 0:tcw], var[:, 0:tcw])
    bcr = row_bcast(P, ctx, var[:, 0:tcw], tcw, bci=1)
    for ft in range(KT8):
        t = tiles[ft]
        P.tt(t, t, bcr, ALU.mult)
        if gcol is not None:
            P.ts(t, t, gcol[:, ft:ft + 1], None, ALU.mult)
        out_cb(ft, t)


def residual_ln_store(P, ctx, fsb, fw, x_hbm, out_hbm, eps=1e-5, ysb=None):
    for ci in range(NCH):
        tc0 = ci * TC
        for ft in range(KT8):
            P.dma(ctx.stage[:, 0:TC], x_hbm[ft * 128:(ft + 1) * 128, tc0:tc0 + TC])
            t = fsb[:, ft * fw + tc0: ft * fw + tc0 + TC]
            P.tt(t, t, ctx.stage[:, 0:TC], ALU.add)

        def store(ft, ap, tc0=tc0):
            P.dma(out_hbm[ft * 128:(ft + 1) * 128, tc0:tc0 + TC], ap)

        ln_chunk(P, ctx, fsb, fw, tc0, TC, eps, store)


def gate_row_bc(P, ctx, gate_tm, ncols, e, tc0, tcw, ident, pst, rowbuf):
    P = P
    for i in range(tcw // 128):
        t = (tc0 + i * 128) // 128
        P.trans(pst[0:1, 0:128], gate_tm[:, t * ncols + e: t * ncols + e + 1], ident[0:128, 0:128])
        P.vcopy(rowbuf[0:1, i * 128:(i + 1) * 128], pst[0:1, 0:128])
    return row_bcast(P, ctx, rowbuf[0:1, 0:tcw], tcw, bci=0)


def attn_core(P, ctx, mode, qsb, k_hbm, v_hbm, oacc, accs, qidx_sb, kidx_sb,
              colfac_sb=None, facK_sb=None, expscale=1.0):
    ksb = P.sb("at_k", [128, KT8 * TC], BF16)
    vsb = P.sb("at_v", [128, 4 * 1024], BF16)
    ssb = P.sb("at_s", [128, TC], BF16)
    asb = P.sb("at_a", [128, TC])
    msk = P.sb("at_m", [128, 4 * TC], BF16)
    qbc = P.sb("at_qbc", [128, TC])
    pso = P.ps("at_pso", [128, TC])
    P.memset(oacc[:], 0.0)
    for qc in range(NCH):
        P.memset(accs[:], 0.0)
        bq = row_bcast(P, ctx, qidx_sb[0:1, qc * TC:(qc + 1) * TC], TC, bci=0)
        P.vcopy(qbc[:], bq)
        for kc in range(NKC):
            for g in range(2):
                for f4 in range(4):
                    ft = g * 4 + f4
                    P.dma(ctx.stage[:, f4 * TC:(f4 + 1) * TC],
                          k_hbm[ft * 128:(ft + 1) * 128, kc * TC:(kc + 1) * TC])
                P.vcopy(ksb[:, g * 4 * TC:(g + 1) * 4 * TC], ctx.stage[:, 0:4 * TC])
            if facK_sb is not None:
                P.tt(ksb[:], ksb[:], facK_sb[:], ALU.mult)
            for tt_ in range(4):
                P.dma(ctx.stage[:, 0:1024],
                      v_hbm[kc * TC + tt_ * 128: kc * TC + (tt_ + 1) * 128, :])
                P.vcopy(vsb[:, tt_ * 1024:(tt_ + 1) * 1024], ctx.stage[:, 0:1024])
            for kt in range(4):
                P.ts(msk[:, kt * TC:(kt + 1) * TC], qbc[:],
                     kidx_sb[:, kc * 4 + kt: kc * 4 + kt + 1], None, ALU.is_ge)
            for h in range(H):
                po = 64 * (h % 2)
                hf = h // 2
                for kt in range(4):
                    pss = ctx.psb[0]
                    P.mm(pss[:, 0:TC],
                         ksb[po:po + 64, hf * TC + kt * 128: hf * TC + kt * 128 + 128],
                         qsb[po:po + 64, hf * NT + qc * TC: hf * NT + (qc + 1) * TC])
                    if mode == "soft":
                        P.act(ssb[:], pss[:, 0:TC], AF.Exp, scale=expscale)
                    else:
                        P.ts(ssb[:], pss[:, 0:TC],
                             colfac_sb[:, h * 64 + qc * 16 + kc: h * 64 + qc * 16 + kc + 1],
                             None, ALU.mult)
                    P.tt(ssb[:], ssb[:], msk[:, kt * TC:(kt + 1) * TC], ALU.mult)
                    P.act(asb[:], ssb[:], AF.Abs)
                    P.tt(accs[:, h * TC:(h + 1) * TC], accs[:, h * TC:(h + 1) * TC],
                         asb[:], ALU.add)
                    P.mm(pso[po:po + 64, 0:TC],
                         vsb[:, kt * 1024 + h * 64: kt * 1024 + h * 64 + 64],
                         ssb[:], start=(kt == 0), stop=(kt == 3))
                od = oacc[po:po + 64, hf * NT + qc * TC: hf * NT + (qc + 1) * TC]
                P.tt(od, od, pso[po:po + 64, 0:TC], ALU.add)
        for h in range(H):
            po = 64 * (h % 2)
            hf = h // 2
            pr = ctx.psr[0]
            P.mm(pr[:, 0:TC], ctx.ones[:, 0:1], accs[:, h * TC:(h + 1) * TC])
            r = ctx.rows[0]
            if mode == "retnet":
                P.ts(r[:, 0:TC], pr[:, 0:TC], 1.0, 50000.0, ALU.max, ALU.min)
            else:
                P.vcopy(r[:, 0:TC], pr[:, 0:TC])
            P.recip(r[:, 0:TC], r[:, 0:TC])
            bc = row_bcast(P, ctx, r[:, 0:TC], TC, nrows=64, bci=1, pbase=po)
            od = oacc[po:po + 64, hf * NT + qc * TC: hf * NT + (qc + 1) * TC]
            P.tt(od, od, bc, ALU.mult)


def b_l0a_proj():
    P = Prog()
    ctx = Ctx(P)
    x_h = P.din("x", [E, NT])
    w_h = {n: P.din(n, [E, E]) for n in ["Wq", "Wk", "Wv", "Wg", "WqP", "WkP"]}
    cos_h = P.din("cos2", [128, NT])
    sin_h = P.din("sin2", [128, NT])
    sgn_h = P.din("sgncol", [128, 1])
    qr_o = P.dout("qr", [E, NT])
    kr_o = P.dout("kr", [E, NT])
    v_o = P.dout("v", [NT, E])
    g_o = P.dout("g", [E, NT])

    xsb = P.sb("xsb", [128, KT8 * NT], BF16)
    aload(P, ctx, xsb, 0, x_h, 0, NT, NT)
    cossb = P.sb("cossb", [128, NT])
    sinsb = P.sb("sinsb", [128, NT])
    sgnsb = P.sb("sgnsb", [128, 1])
    P.dma(cossb[:], cos_h[:])
    P.dma(sinsb[:], sin_h[:])
    P.dma(sgnsb[:], sgn_h[:])
    qt = P.sb("qt", [128, NT])
    rt = P.sb("rt", [128, NT])

    scal = float(E) ** -0.5

    def proj_pair(wn, wpn, out_hbm, kscale):
        w = wload(P, ctx, "w_" + wn, w_h[wn], E, E)
        wp = wload(P, ctx, "w_" + wpn, w_h[wpn], E, E)
        for ft in range(KT8):
            for ci in range(NCH):
                def ev_q(nt, ps, ci=ci):
                    P.ts(qt[:, ci * TC:(ci + 1) * TC], ps, kscale, None, ALU.mult)

                def ev_r(nt, ps, ci=ci):
                    if kscale == 1.0:
                        P.ts(rt[:, ci * TC:(ci + 1) * TC], ps, sgnsb[:, 0:1], None, ALU.mult)
                    else:
                        P.ts(rt[:, ci * TC:(ci + 1) * TC], ps, sgnsb[:, 0:1], kscale,
                             ALU.mult, ALU.mult)

                for kt in range(KT8):
                    P.mm(ctx.psb[0][:, 0:TC], w[:, kt * E + ft * 128: kt * E + ft * 128 + 128],
                         xsb[:, kt * NT + ci * TC: kt * NT + (ci + 1) * TC],
                         start=(kt == 0), stop=(kt == KT8 - 1))
                ev_q(ft, ctx.psb[0][:, 0:TC])
                for kt in range(KT8):
                    P.mm(ctx.psb[1][:, 0:TC], wp[:, kt * E + ft * 128: kt * E + ft * 128 + 128],
                         xsb[:, kt * NT + ci * TC: kt * NT + (ci + 1) * TC],
                         start=(kt == 0), stop=(kt == KT8 - 1))
                ev_r(ft, ctx.psb[1][:, 0:TC])
            P.tt(qt[:], qt[:], cossb[:], ALU.mult)
            P.tt(rt[:], rt[:], sinsb[:], ALU.mult)
            P.tt(qt[:], qt[:], rt[:], ALU.add)
            P.dma(out_hbm[ft * 128:(ft + 1) * 128, :], qt[:])

    proj_pair("Wq", "WqP", qr_o, 1.0)
    proj_pair("Wk", "WkP", kr_o, scal)

    wg = wload(P, ctx, "w_g", w_h["Wg"], E, E)
    for ci in range(NCH):
        def ev_g(nt, ps, ci=ci):
            P.vcopy(qt[:, 0:TC], ps)
            P.dma(g_o[nt * 128:(nt + 1) * 128, ci * TC:(ci + 1) * TC], qt[:, 0:TC])

        mm_chain(P, ctx, xsb, NT, wg, E, E, ci * TC, TC, ev_g)

    wv = wload(P, ctx, "w_v", w_h["Wv"], E, E)
    for tti in range(NT // 128):
        for nc0 in range(0, E, TC):
            for kt in range(KT8):
                P.mm(ctx.psb[2][:, 0:TC],
                     xsb[:, kt * NT + tti * 128: kt * NT + (tti + 1) * 128],
                     wv[:, kt * E + nc0: kt * E + nc0 + TC],
                     start=(kt == 0), stop=(kt == KT8 - 1))
            P.vcopy(qt[:, 0:TC], ctx.psb[2][:, 0:TC])
            P.dma(v_o[tti * 128:(tti + 1) * 128, nc0:nc0 + TC], qt[:, 0:TC])
    return P.build()


def b_l0a_attn():
    P = Prog()
    ctx = Ctx(P, stage_w=2048)
    q_h = P.din("qr", [E, NT])
    k_h = P.din("krF", [E, S])
    v_h = P.din("vF", [S, E])
    g_h = P.din("g", [E, NT])
    x_h = P.din("x", [E, NT])
    facQ_h = P.din("facQ", [E, NT])
    facK_h = P.din("facK", [E, TC])
    colfac_h = P.din("colfac", [128, H * 64])
    qidx_h = P.din("qidx", [1, NT])
    kidx_h = P.din("kidx", [128, NKC * 4])
    out_h = P.dout("xo", [E, NT])

    qsb = P.sb("qsb", [128, KT8 * NT], BF16)
    for ft in range(KT8):
        for hv in range(2):
            c0 = hv * 1024
            P.dma(ctx.stage[:, 0:1024], q_h[ft * 128:(ft + 1) * 128, c0:c0 + 1024])
            P.dma(ctx.stage[:, 1024:2048], facQ_h[ft * 128:(ft + 1) * 128, c0:c0 + 1024])
            P.tt(ctx.stage[:, 0:1024], ctx.stage[:, 0:1024], ctx.stage[:, 1024:2048],
                 ALU.mult)
            P.vcopy(qsb[:, ft * NT + c0: ft * NT + c0 + 1024], ctx.stage[:, 0:1024])
    facKsb = P.sb("facKsb", [128, KT8 * TC], BF16)
    for ft in range(KT8):
        P.dma(ctx.stage[:, 0:TC], facK_h[ft * 128:(ft + 1) * 128, :])
        P.vcopy(facKsb[:, ft * TC:(ft + 1) * TC], ctx.stage[:, 0:TC])
    colsb = P.sb("colsb", [128, H * 64])
    P.dma(colsb[:], colfac_h[:])
    qidx = P.sb("qidxsb", [1, NT])
    P.dma(qidx[:], qidx_h[:])
    kidx = P.sb("kidxsb", [128, NKC * 4])
    P.dma(kidx[:], kidx_h[:])

    oacc = P.sb("oacc", [128, KT8 * NT])
    accs = P.sb("accs", [128, H * TC])
    attn_core(P, ctx, "retnet", qsb, k_h, v_h, oacc, accs, qidx, kidx,
              colfac_sb=colsb, facK_sb=facKsb)

    for qc in range(NCH):
        for h in range(H):
            po = 64 * (h % 2)
            hf = h // 2
            od = oacc[po:po + 64, hf * NT + qc * TC: hf * NT + (qc + 1) * TC]
            P.act(ctx.tmps[0][po:po + 64, 0:TC], od, AF.Square)
            pr = ctx.psr[0]
            P.mm(pr[:, 0:TC], ctx.ones[po:po + 64, 0:1], ctx.tmps[0][po:po + 64, 0:TC])
            r = ctx.rows[0]
            P.ts(r[:, 0:TC], pr[:, 0:TC], 1.0 / 64.0, 1e-6, ALU.mult, ALU.add)
            P.act(r[:, 0:TC], r[:, 0:TC], AF.Sqrt)
            P.recip(r[:, 0:TC], r[:, 0:TC])
            bc = row_bcast(P, ctx, r[:, 0:TC], TC, nrows=64, bci=1, pbase=po)
            P.tt(od, od, bc, ALU.mult)
    for ft in range(KT8):
        P.dma(ctx.stage[:, 0:NT], g_h[ft * 128:(ft + 1) * 128, :])
        P.act(ctx.stage[:, 0:NT], ctx.stage[:, 0:NT], AF.Silu)
        P.tt(oacc[:, ft * NT:(ft + 1) * NT], oacc[:, ft * NT:(ft + 1) * NT],
             ctx.stage[:, 0:NT], ALU.mult)
    residual_ln_store(P, ctx, oacc, NT, x_h, out_h)
    return P.build()


def b_soft_attn(biases, w_out_name, final_ln_gb):
    P = Prog()
    ctx = Ctx(P, stage_w=2048)
    q_h = P.din("q", [E, NT])
    k_h = P.din("kF", [E, S])
    v_h = P.din("vF", [S, E])
    x_h = P.din("x", [E, NT])
    wo_h = P.din("Wo", [E, E])
    bo_h = P.din("bo", [128, KT8])
    qidx_h = P.din("qidx", [1, NT])
    kidx_h = P.din("kidx", [128, NKC * 4])
    if final_ln_gb:
        lng_h = P.din("lng", [128, KT8])
        lnb_h = P.din("lnb", [128, KT8])
    out_h = P.dout("xo", [E, NT])

    qsb = P.sb("qsb", [128, KT8 * NT], BF16)
    aload(P, ctx, qsb, 0, q_h, 0, NT, NT)
    qidx = P.sb("qidxsb", [1, NT])
    P.dma(qidx[:], qidx_h[:])
    kidx = P.sb("kidxsb", [128, NKC * 4])
    P.dma(kidx[:], kidx_h[:])
    oacc = P.sb("oacc", [128, KT8 * NT])
    accs = P.sb("accs", [128, H * TC])
    attn_core(P, ctx, "soft", qsb, k_h, v_h, oacc, accs, qidx, kidx, expscale=0.125)

    for ft in range(KT8):
        P.vcopy(qsb[:, ft * NT:(ft + 1) * NT], oacc[:, ft * NT:(ft + 1) * NT])
    wo = wload(P, ctx, "w_o", wo_h, E, E)
    bo = P.sb("bosb", [128, KT8])
    P.dma(bo[:], bo_h[:])
    if final_ln_gb:
        lng = P.sb("lngsb", [128, KT8])
        lnb = P.sb("lnbsb", [128, KT8])
        P.dma(lng[:], lng_h[:])
        P.dma(lnb[:], lnb_h[:])
    for ci in range(NCH):
        def ev(nt, ps, ci=ci):
            d = oacc[:, nt * NT + ci * TC: nt * NT + ci * TC + TC]
            P.ts(d, ps, bo[:, nt:nt + 1], None, ALU.add)

        mm_chain(P, ctx, qsb, NT, wo, E, E, ci * TC, TC, ev)
    if final_ln_gb:
        for ci in range(NCH):
            tc0 = ci * TC
            for ft in range(KT8):
                P.dma(ctx.stage[:, 0:TC], x_h[ft * 128:(ft + 1) * 128, tc0:tc0 + TC])
                t = oacc[:, ft * NT + tc0: ft * NT + tc0 + TC]
                P.tt(t, t, ctx.stage[:, 0:TC], ALU.add)

            def store(ft, ap, tc0=tc0):
                P.dma(out_h[ft * 128:(ft + 1) * 128, tc0:tc0 + TC], ap)

            ln_chunk(P, ctx, oacc, NT, tc0, TC, 1e-6, store, gcol=lng, bcol=lnb)
    else:
        residual_ln_store(P, ctx, oacc, NT, x_h, out_h)
    return P.build()


def b_qkv_proj(with_bias):
    P = Prog()
    ctx = Ctx(P)
    x_h = P.din("x", [E, NT])
    wq_h = P.din("Wq", [E, E])
    wk_h = P.din("Wk", [E, E])
    wv_h = P.din("Wv", [E, E])
    if with_bias:
        bq_h = P.din("bq", [128, KT8])
        bk_h = P.din("bk", [128, KT8])
        bv_h = P.din("bv", [128, KT8])
    q_o = P.dout("q", [E, NT])
    k_o = P.dout("k", [E, NT])
    v_o = P.dout("v", [NT, E])

    xsb = P.sb("xsb", [128, KT8 * NT], BF16)
    aload(P, ctx, xsb, 0, x_h, 0, NT, NT)
    ob = P.sb("ob", [128, TC])
    if with_bias:
        bqs = P.sb("bqs", [128, KT8]); P.dma(bqs[:], bq_h[:])
        bks = P.sb("bks", [128, KT8]); P.dma(bks[:], bk_h[:])
        bvs = P.sb("bvs", [128, KT8]); P.dma(bvs[:], bv_h[:])

    def projfm(w_h, out_hbm, bsb, name):
        w = wload(P, ctx, name, w_h, E, E)
        for ci in range(NCH):
            def ev(nt, ps, ci=ci):
                if bsb is not None:
                    P.ts(ob[:, 0:TC], ps, bsb[:, nt:nt + 1], None, ALU.add)
                else:
                    P.vcopy(ob[:, 0:TC], ps)
                P.dma(out_hbm[nt * 128:(nt + 1) * 128, ci * TC:(ci + 1) * TC], ob[:, 0:TC])

            mm_chain(P, ctx, xsb, NT, w, E, E, ci * TC, TC, ev)

    projfm(wq_h, q_o, bqs if with_bias else None, "w_q")
    projfm(wk_h, k_o, bks if with_bias else None, "w_k")
    wv = wload(P, ctx, "w_v", wv_h, E, E)
    if with_bias:
        bvrow_h = P.din("bvrow", [1, E])
        brow = P.sb("brow", [1, E])
        P.dma(brow[:], bvrow_h[:])
    for tti in range(NT // 128):
        for j, nc0 in enumerate(range(0, E, TC)):
            for kt in range(KT8):
                P.mm(ctx.psb[2][:, 0:TC],
                     xsb[:, kt * NT + tti * 128: kt * NT + (tti + 1) * 128],
                     wv[:, kt * E + nc0: kt * E + nc0 + TC],
                     start=(kt == 0), stop=(kt == KT8 - 1))
            if with_bias:
                bc = row_bcast(P, ctx, brow[0:1, nc0:nc0 + TC], TC, bci=0)
                P.tt(ob[:, 0:TC], ctx.psb[2][:, 0:TC], bc, ALU.add)
            else:
                P.vcopy(ob[:, 0:TC], ctx.psb[2][:, 0:TC])
            P.dma(v_o[tti * 128:(tti + 1) * 128, nc0:nc0 + TC], ob[:, 0:TC])
    return P.build()


def topk_iter2(P, ctx, work, selmask, k):
    c0 = ctx.cols[0]
    eq = ctx.tmps[0][:, 0:work.shape[-1]]
    for _ in range(k):
        P.red(c0[:, 0:1], work, ALU.max)
        P.ts(eq, work, c0[:, 0:1], None, ALU.is_equal)
        P.tt(selmask, selmask, eq, ALU.max)
        P.stt(work, eq, -1e30, work, ALU.mult, ALU.add)


def b_l0f():
    P = Prog()
    ctx = Ctx(P, stage_w=2048)
    x_h = P.din("x", [E, NT])
    gw_h = P.din("gwT", [E, 32])
    eg_h = P.din("egT", [32, E, 512])
    eu_h = P.din("euT", [32, E, 512])
    ed_h = P.din("edT", [32, 512, E])
    sg_h = P.din("sgT", [E, 512])
    su_h = P.din("suT", [E, 512])
    sd_h = P.din("sdT", [512, E])
    G_h = P.din("G", [32, 8])
    G2_h = P.din("G2", [8, 32])
    ident_h = P.din("ident", [128, 128])
    out_h = P.dout("xo", [E, NT])

    xsb = P.sb("xsb", [128, KT8 * NT], BF16)
    aload(P, ctx, xsb, 0, x_h, 0, NT, NT)
    ident = P.sb("identsb", [128, 128])
    P.dma(ident[:], ident_h[:])
    Gsb = P.sb("Gsb", [32, 8])
    P.dma(Gsb[:], G_h[:])
    G2sb = P.sb("G2sb", [8, 32])
    P.dma(G2sb[:], G2_h[:])

    wg = wload(P, ctx, "w_gate", gw_h, E, 32)
    scfm = P.sb("scfm", [32, NT])
    for ci in range(NCH):
        ps = ctx.psb[0]
        for kt in range(KT8):
            P.mm(ps[0:32, 0:TC], wg[:, kt * 32:(kt + 1) * 32],
                 xsb[:, kt * NT + ci * TC:(kt * NT + (ci + 1) * TC)],
                 start=(kt == 0), stop=(kt == KT8 - 1))
        P.op("scalar", lambda e, ci=ci, ps=ps: e.activation(
            scfm[0:32, ci * TC:(ci + 1) * TC], ps[0:32, 0:TC], AF.Sigmoid))
    comb_tm = P.sb("compb", [128, (NT // 128) * 32])
    gsT = P.sb("gsT", [8, TC])
    gmT = P.sb("gmT", [8, TC])
    gs_tm = P.sb("gs_tm", [128, 8])
    gm_tm = P.sb("gm_tm", [128, 8])
    emT = P.sb("emT", [32, TC])
    ms_tm = P.sb("ms_tm", [128, 32])
    sel_tm = P.sb("sel_tm", [128, 32])
    pst = ctx.psb[3]
    for ci in range(NCH):
        P.mm(ctx.psb[1][0:8, 0:TC], Gsb[:], scfm[0:32, ci * TC:(ci + 1) * TC])
        P.vcopy(gsT[:, 0:TC], ctx.psb[1][0:8, 0:TC])
        for t4 in range(4):
            t = ci * 4 + t4
            P.trans(pst[0:128, 0:8], gsT[0:8, t4 * 128:(t4 + 1) * 128], ident[0:8, 0:8])
            P.vcopy(gs_tm[:], pst[0:128, 0:8])
            P.memset(gm_tm[:], 0.0)
            topk_iter2(P, ctx, gs_tm[:], gm_tm[:], 4)
            P.trans(pst[0:8, 0:128], gm_tm[:], ident[0:128, 0:128])
            P.vcopy(gmT[0:8, t4 * 128:(t4 + 1) * 128], pst[0:8, 0:128])
        P.mm(ctx.psb[1][0:32, 0:TC], G2sb[:], gmT[0:8, 0:TC])
        P.tt(emT[0:32, 0:TC], ctx.psb[1][0:32, 0:TC],
             scfm[0:32, ci * TC:(ci + 1) * TC], ALU.mult)
        for t4 in range(4):
            t = ci * 4 + t4
            P.trans(pst[0:128, 0:32], emT[0:32, t4 * 128:(t4 + 1) * 128], ident[0:32, 0:32])
            P.vcopy(ms_tm[:], pst[0:128, 0:32])
            P.memset(sel_tm[:], 0.0)
            c_ms = ctx.tmps[1][:, 0:32]
            P.vcopy(c_ms, ms_tm[:])
            topk_iter2(P, ctx, c_ms, sel_tm[:], 8)
            P.tt(ms_tm[:], ms_tm[:], sel_tm[:], ALU.mult)
            P.red(ctx.cols[1][:, 0:1], ms_tm[:], ALU.add)
            P.ts(ctx.cols[1][:, 0:1], ctx.cols[1][:, 0:1], 1e-12, None, ALU.max)
            P.recip(ctx.cols[1][:, 0:1], ctx.cols[1][:, 0:1])
            P.ts(comb_tm[:, t * 32:(t + 1) * 32], ms_tm[:], ctx.cols[1][:, 0:1],
                 None, ALU.mult)

    acc = P.sb("acc", [128, KT8 * NT])
    P.memset(acc[:], 0.0)
    wga = P.sb("wga", [128, KT8 * 512], BF16)
    wua = P.sb("wua", [128, KT8 * 512], BF16)
    wda = P.sb("wda", [128, 4 * E], BF16)
    hsb = P.sb("hsb", [128, 4 * TC], BF16)
    growb = P.sb("growb", [1, TC])

    def expert(eidx):
        shared = eidx < 0
        if shared:
            wfill(P, ctx, wga, sg_h, E, 512)
            wfill(P, ctx, wua, su_h, E, 512)
            wfill(P, ctx, wda, sd_h, 512, E)
        else:
            wfill(P, ctx, wga, eg_h[eidx], E, 512)
            wfill(P, ctx, wua, eu_h[eidx], E, 512)
            wfill(P, ctx, wda, ed_h[eidx], 512, E)
        for ci in range(NCH):
            for it in range(4):
                for kt in range(KT8):
                    P.mm(ctx.psb[0][:, 0:TC],
                         wga[:, kt * 512 + it * 128: kt * 512 + it * 128 + 128],
                         xsb[:, kt * NT + ci * TC: kt * NT + (ci + 1) * TC],
                         start=(kt == 0), stop=(kt == KT8 - 1))
                    P.mm(ctx.psb[1][:, 0:TC],
                         wua[:, kt * 512 + it * 128: kt * 512 + it * 128 + 128],
                         xsb[:, kt * NT + ci * TC: kt * NT + (ci + 1) * TC],
                         start=(kt == 0), stop=(kt == KT8 - 1))
                P.act(ctx.tmps[0][:, 0:TC], ctx.psb[0][:, 0:TC], AF.Silu)
                P.tt(hsb[:, it * TC:(it + 1) * TC], ctx.tmps[0][:, 0:TC],
                     ctx.psb[1][:, 0:TC], ALU.mult)
            if not shared:
                bc = gate_row_bc(P, ctx, comb_tm, 32, eidx, ci * TC, TC, ident,
                                 ctx.psb[3], growb)
            for dt in range(KT8):
                for kt in range(4):
                    P.mm(ctx.psb[2][:, 0:TC],
                         wda[:, kt * E + dt * 128: kt * E + dt * 128 + 128],
                         hsb[:, kt * TC:(kt + 1) * TC],
                         start=(kt == 0), stop=(kt == 3))
                a = acc[:, dt * NT + ci * TC: dt * NT + (ci + 1) * TC]
                if shared:
                    P.tt(a, a, ctx.psb[2][:, 0:TC], ALU.add)
                else:
                    P.tt(ctx.tmps[0][:, 0:TC], ctx.psb[2][:, 0:TC], bc, ALU.mult)
                    P.tt(a, a, ctx.tmps[0][:, 0:TC], ALU.add)

    for e in range(32):
        expert(e)
    expert(-1)
    residual_ln_store(P, ctx, acc, NT, x_h, out_h)
    return P.build()


def b_l1f_moe():
    P = Prog()
    ctx = Ctx(P, stage_w=2048)
    o1_h = P.din("out1", [E, NT])
    x3_h = P.din("x3", [E, NT])
    wr_h = P.din("Wr", [E, 8])
    we1_h = P.din("We1", [8, E, 4096])
    be1_h = P.din("be1r", [8, 128, 32])
    we2_h = P.din("We2", [8, 4096, E])
    be2_h = P.din("be2r", [8, 128, 8])
    g2_h = P.din("ln2g", [128, KT8])
    b2_h = P.din("ln2b", [128, KT8])
    ident_h = P.din("ident", [128, 128])
    macc_h = P.dout("macc", [E, NT])
    out_h = P.dout("xo", [E, NT])

    ident = P.sb("identsb", [128, 128])
    P.dma(ident[:], ident_h[:])
    xc = P.sb("xc", [128, KT8 * TC], BF16)
    gates_tm = P.sb("gates_tm", [128, (NT // 128) * 8])
    wr = wload(P, ctx, "w_r", wr_h, E, 8)
    lgT = P.sb("lgT", [8, TC])
    lg_tm = P.sb("lg_tm", [128, 8])
    sel_tm = P.sb("sel_tm", [128, 8])
    pst = ctx.psb[3]
    for ci in range(NCH):
        aload(P, ctx, xc, 0, o1_h, ci * TC, TC, TC)
        ps = ctx.psb[0]
        for kt in range(KT8):
            P.mm(ps[0:8, 0:TC], wr[:, kt * 8:(kt + 1) * 8],
                 xc[:, kt * TC:(kt + 1) * TC], start=(kt == 0), stop=(kt == KT8 - 1))
        P.vcopy(lgT[:, 0:TC], ps[0:8, 0:TC])
        for t4 in range(4):
            t = ci * 4 + t4
            P.trans(pst[0:128, 0:8], lgT[0:8, t4 * 128:(t4 + 1) * 128], ident[0:8, 0:8])
            P.vcopy(lg_tm[:], pst[0:128, 0:8])
            P.red(ctx.cols[0][:, 0:1], lg_tm[:], ALU.max)
            P.ts(ctx.cols[0][:, 0:1], ctx.cols[0][:, 0:1], -1.0, None, ALU.mult)
            P.op("scalar", lambda e: e.activation(lg_tm[:], lg_tm[:], AF.Exp,
                                                  bias=ctx.cols[0][:, 0:1], scale=1.0))
            P.red(ctx.cols[1][:, 0:1], lg_tm[:], ALU.add)
            P.recip(ctx.cols[1][:, 0:1], ctx.cols[1][:, 0:1])
            P.ts(lg_tm[:], lg_tm[:], ctx.cols[1][:, 0:1], None, ALU.mult)
            P.memset(sel_tm[:], 0.0)
            c_p = ctx.tmps[1][:, 0:8]
            P.vcopy(c_p, lg_tm[:])
            topk_iter2(P, ctx, c_p, sel_tm[:], 2)
            P.op("scalar", lambda e: e.activation(ctx.tmps[0][:, 0:8], lg_tm[:], AF.Exp))
            P.tt(ctx.tmps[0][:, 0:8], ctx.tmps[0][:, 0:8], sel_tm[:], ALU.mult)
            P.red(ctx.cols[1][:, 0:1], ctx.tmps[0][:, 0:8], ALU.add)
            P.recip(ctx.cols[1][:, 0:1], ctx.cols[1][:, 0:1])
            P.ts(gates_tm[:, t * 8:(t + 1) * 8], ctx.tmps[0][:, 0:8],
                 ctx.cols[1][:, 0:1], None, ALU.mult)

    we1 = P.sb("we1", [128, KT8 * 4096], BF16)
    we2 = P.sb("we2", [128, 32 * E], BF16)
    hsb = P.sb("hsb", [128, 32 * TC], BF16)
    b1c = P.sb("b1c", [128, 32])
    b2c = P.sb("b2c", [128, 8])
    growb = P.sb("growb", [1, TC])
    for e in range(8):
        wfill(P, ctx, we1, we1_h[e], E, 4096)
        wfill(P, ctx, we2, we2_h[e], 4096, E)
        P.dma(b1c[:], be1_h[e])
        P.dma(b2c[:], be2_h[e])
        for ci in range(NCH):
            aload(P, ctx, xc, 0, o1_h, ci * TC, TC, TC)
            for nt in range(32):
                for kt in range(KT8):
                    P.mm(ctx.psb[0][:, 0:TC],
                         we1[:, kt * 4096 + nt * 128: kt * 4096 + nt * 128 + 128],
                         xc[:, kt * TC:(kt + 1) * TC],
                         start=(kt == 0), stop=(kt == KT8 - 1))
                P.op("scalar", lambda e2, nt=nt: e2.activation(
                    hsb[:, nt * TC:(nt + 1) * TC], ctx.psb[0][:, 0:TC], AF.Gelu,
                    bias=b1c[:, nt:nt + 1], scale=1.0))
            bc = gate_row_bc(P, ctx, gates_tm, 8, e, ci * TC, TC, ident,
                             ctx.psb[3], growb)
            for dt in range(KT8):
                for kt in range(32):
                    P.mm(ctx.psb[2][:, 0:TC],
                         we2[:, kt * E + dt * 128: kt * E + dt * 128 + 128],
                         hsb[:, kt * TC:(kt + 1) * TC],
                         start=(kt == 0), stop=(kt == 31))
                P.ts(ctx.tmps[0][:, 0:TC], ctx.psb[2][:, 0:TC], b2c[:, dt:dt + 1],
                     None, ALU.add)
                P.tt(ctx.tmps[0][:, 0:TC], ctx.tmps[0][:, 0:TC], bc, ALU.mult)
                dst = macc_h[dt * 128:(dt + 1) * 128, ci * TC:(ci + 1) * TC]
                if e == 0:
                    P.dma(dst, ctx.tmps[0][:, 0:TC])
                else:
                    P.dma(dst, ctx.tmps[0][:, 0:TC], q="gpsimd", accum_op=ALU.add)

    g2c = P.sb("g2c", [128, KT8])
    b2cc = P.sb("b2cc", [128, KT8])
    P.dma(g2c[:], g2_h[:])
    P.dma(b2cc[:], b2_h[:])
    TH = 256
    fb = P.sb("fb", [128, KT8 * TH])
    for ci in range(NT // TH):
        for ft in range(KT8):
            P.dma(ctx.stage[:, 0:TH], macc_h[ft * 128:(ft + 1) * 128,
                                             ci * TH:(ci + 1) * TH])
            P.dma(ctx.tmps[0][:, 0:TH], o1_h[ft * 128:(ft + 1) * 128,
                                             ci * TH:(ci + 1) * TH])
            P.tt(fb[:, ft * TH:(ft + 1) * TH], ctx.stage[:, 0:TH],
                 ctx.tmps[0][:, 0:TH], ALU.add)

        def add_x3(ft, ap, ci=ci):
            P.dma(ctx.stage[:, 0:TH], x3_h[ft * 128:(ft + 1) * 128,
                                           ci * TH:(ci + 1) * TH])
            P.tt(ap, ap, ctx.stage[:, 0:TH], ALU.add)

        ln_chunk(P, ctx, fb, TH, 0, TH, 1e-6, add_x3, gcol=g2c, bcol=b2cc)

        def store(ft, ap, ci=ci):
            P.dma(out_h[ft * 128:(ft + 1) * 128, ci * TH:(ci + 1) * TH], ap)

        ln_chunk(P, ctx, fb, TH, 0, TH, 1e-5, store)
    return P.build()


def b_l2f_a():
    P = Prog()
    ctx = Ctx(P, stage_w=2048)
    x_h = P.din("x", [E, NT])
    g_h = P.din("lng", [128, KT8])
    b_h = P.din("lnb", [128, KT8])
    q1_h = P.din("fc1q", [E, 4096], I32)
    s1_h = P.din("fc1s", [8, 4096])
    b1_h = P.din("b1r", [128, 32])
    y_o = P.dout("y", [4096, NT], BF16)

    gc = P.sb("gc", [128, KT8]); P.dma(gc[:], g_h[:])
    bc_ = P.sb("bc_", [128, KT8]); P.dma(bc_[:], b_h[:])
    b1c = P.sb("b1c", [128, 32]); P.dma(b1c[:], b1_h[:])
    w1 = P.sb("w1", [128, KT8 * 4096], BF16)
    sti = P.sb("sti", [128, 2048], I32)
    srow = P.sb("srow", [1, 4096])
    for kt in range(KT8):
        P.dma(srow[:], s1_h[kt:kt + 1, :])
        for half in range(2):
            c0 = half * 2048
            P.dma(sti[:], q1_h[kt * 128:(kt + 1) * 128, c0:c0 + 2048])
            P.scopy(ctx.stage[:, 0:2048], sti[:])
            P.ts(ctx.stage[:, 0:2048], ctx.stage[:, 0:2048], -128.0, None, ALU.add)
            for p in range(4):
                bcx = row_bcast(P, ctx, srow[0:1, c0 + p * TC: c0 + (p + 1) * TC], TC, bci=0)
                P.tt(w1[:, kt * 4096 + c0 + p * TC: kt * 4096 + c0 + (p + 1) * TC],
                     ctx.stage[:, p * TC:(p + 1) * TC], bcx, ALU.mult)
    fb = P.sb("fb", [128, KT8 * TC])
    hc = P.sb("hc", [128, KT8 * TC], BF16)
    yc = P.sb("yc", [128, TC], BF16)
    for ci in range(NCH):
        for ft in range(KT8):
            P.dma(fb[:, ft * TC:(ft + 1) * TC],
                  x_h[ft * 128:(ft + 1) * 128, ci * TC:(ci + 1) * TC])

        def tobf(ft, ap):
            P.vcopy(hc[:, ft * TC:(ft + 1) * TC], ap)

        ln_chunk(P, ctx, fb, TC, 0, TC, 1e-5, tobf, gcol=gc, bcol=bc_)
        for nt in range(32):
            for kt in range(KT8):
                P.mm(ctx.psb[0][:, 0:TC],
                     w1[:, kt * 4096 + nt * 128: kt * 4096 + nt * 128 + 128],
                     hc[:, kt * TC:(kt + 1) * TC],
                     start=(kt == 0), stop=(kt == KT8 - 1))
            P.op("scalar", lambda e, nt=nt: e.activation(
                yc[:, 0:TC], ctx.psb[0][:, 0:TC], AF.Silu,
                bias=b1c[:, nt:nt + 1], scale=1.0))
            P.dma(y_o[nt * 128:(nt + 1) * 128, ci * TC:(ci + 1) * TC], yc[:, 0:TC])
    return P.build()


def b_l2f_b():
    P = Prog()
    ctx = Ctx(P, stage_w=2048)
    x_h = P.din("x", [E, NT])
    y_h = P.din("y", [4096, NT], BF16)
    q2_h = P.din("fc2q", [4096, E], I32)
    s2_h = P.din("fc2s", [32, E])
    b2_h = P.din("b2r", [128, 8])
    out_h = P.dout("xo", [E, NT])

    b2c = P.sb("b2c", [128, 8]); P.dma(b2c[:], b2_h[:])
    w2 = P.sb("w2", [128, 32 * E], BF16)
    sti = P.sb("sti", [128, 2048], I32)
    srow = P.sb("srow", [1, E])
    for kt in range(32):
        P.dma(srow[:], s2_h[kt:kt + 1, :])
        P.dma(sti[:, 0:E], q2_h[kt * 128:(kt + 1) * 128, :])
        P.scopy(ctx.stage[:, 0:E], sti[:, 0:E])
        P.ts(ctx.stage[:, 0:E], ctx.stage[:, 0:E], -128.0, None, ALU.add)
        for p in range(2):
            bcx = row_bcast(P, ctx, srow[0:1, p * TC:(p + 1) * TC], TC, bci=0)
            P.tt(w2[:, kt * E + p * TC: kt * E + (p + 1) * TC],
                 ctx.stage[:, p * TC:(p + 1) * TC], bcx, ALU.mult)
    yc = P.sb("ycb", [128, 32 * TC], BF16)
    fb = P.sb("fb", [128, KT8 * TC])
    for ci in range(NCH):
        for nt in range(32):
            P.dma(yc[:, nt * TC:(nt + 1) * TC],
                  y_h[nt * 128:(nt + 1) * 128, ci * TC:(ci + 1) * TC])
        for dt in range(KT8):
            for kt in range(32):
                P.mm(ctx.psb[0][:, 0:TC],
                     w2[:, kt * E + dt * 128: kt * E + dt * 128 + 128],
                     yc[:, kt * TC:(kt + 1) * TC],
                     start=(kt == 0), stop=(kt == 31))
            P.ts(ctx.tmps[0][:, 0:TC], ctx.psb[0][:, 0:TC], b2c[:, dt:dt + 1],
                 None, ALU.add)
            P.dma(ctx.stage[:, 0:TC], x_h[dt * 128:(dt + 1) * 128,
                                          ci * TC:(ci + 1) * TC])
            P.ts(ctx.tmps[0][:, 0:TC], ctx.tmps[0][:, 0:TC], 0.5, None, ALU.mult)
            P.stt(fb[:, dt * TC:(dt + 1) * TC], ctx.stage[:, 0:TC], 2.0,
                  ctx.tmps[0][:, 0:TC], ALU.mult, ALU.add)

        def store(ft, ap, ci=ci):
            P.dma(out_h[ft * 128:(ft + 1) * 128, ci * TC:(ci + 1) * TC], ap)

        ln_chunk(P, ctx, fb, TC, 0, TC, 1e-5, store)
    return P.build()


def b_l3a_proj():
    P = Prog()
    ctx = Ctx(P)
    x_h = P.din("x", [E, NT])
    w_h = P.din("wqkv", [E, 3 * E])
    q_o = P.dout("q", [E, NT])
    k_o = P.dout("k", [E, NT])
    v_o = P.dout("v", [NT, E])
    ql_o = P.dout("qlp", [E, 16])
    kl_o = P.dout("klp", [E, 16])

    xsb = P.sb("xsb", [128, KT8 * NT], BF16)
    aload(P, ctx, xsb, 0, x_h, 0, NT, NT)
    w = wload(P, ctx, "w_qkv", w_h, E, 3 * E)
    ob = P.sb("ob", [128, TC])
    qlsb = P.sb("qlsb", [128, KT8 * 16])
    klsb = P.sb("klsb", [128, KT8 * 16])

    def fmproj(col0, out_hbm, scale, lsb):
        for ci in range(NCH):
            for nt in range(KT8):
                for kt in range(KT8):
                    P.mm(ctx.psb[0][:, 0:TC],
                         w[:, kt * 3 * E + col0 + nt * 128: kt * 3 * E + col0 + nt * 128 + 128],
                         xsb[:, kt * NT + ci * TC: kt * NT + (ci + 1) * TC],
                         start=(kt == 0), stop=(kt == KT8 - 1))
                P.ts(ob[:, 0:TC], ctx.psb[0][:, 0:TC], scale, None, ALU.mult)
                P.dma(out_hbm[nt * 128:(nt + 1) * 128, ci * TC:(ci + 1) * TC], ob[:, 0:TC])
                if lsb is not None:
                    for j in range(4):
                        P.red(lsb[:, nt * 16 + ci * 4 + j: nt * 16 + ci * 4 + j + 1],
                              ob[:, j * 128:(j + 1) * 128], ALU.add)

    fmproj(0, q_o, 0.125, qlsb)
    fmproj(E, k_o, 1.0, klsb)
    P.ts(qlsb[:], qlsb[:], 1.0 / 128.0, None, ALU.mult)
    P.ts(klsb[:], klsb[:], 1.0 / 128.0, None, ALU.mult)
    for nt in range(KT8):
        P.dma(ql_o[nt * 128:(nt + 1) * 128, :], qlsb[:, nt * 16:(nt + 1) * 16])
        P.dma(kl_o[nt * 128:(nt + 1) * 128, :], klsb[:, nt * 16:(nt + 1) * 16])
    for tti in range(NT // 128):
        for j, nc0 in enumerate(range(0, E, TC)):
            for kt in range(KT8):
                P.mm(ctx.psb[2][:, 0:TC],
                     xsb[:, kt * NT + tti * 128: kt * NT + (tti + 1) * 128],
                     w[:, kt * 3 * E + 2 * E + nc0: kt * 3 * E + 2 * E + nc0 + TC],
                     start=(kt == 0), stop=(kt == KT8 - 1))
            P.vcopy(ob[:, 0:TC], ctx.psb[2][:, 0:TC])
            P.dma(v_o[tti * 128:(tti + 1) * 128, nc0:nc0 + TC], ob[:, 0:TC])
    return P.build()


def b_l3a_core():
    P = Prog()
    ctx = Ctx(P, stage_w=2048)
    q_h = P.din("q", [E, NT])
    k_h = P.din("kF", [E, S])
    v_h = P.din("vF", [S, E])
    ql_h = P.din("Ql", [E, 64])
    kl_h = P.din("Kl", [E, 64])
    x_h = P.din("x", [E, NT])
    wp_h = P.din("wproj", [E, E])
    ident_h = P.din("ident", [128, 128])
    e7_h = P.din("eye7", [64, 64])
    e15_h = P.din("eye15", [64, 64])
    e13_h = P.din("eye13", [64, 64])
    out_h = P.dout("xo", [E, NT])

    ident = P.sb("identsb", [128, 128]); P.dma(ident[:], ident_h[:])
    e7 = P.sb("e7", [64, 64]); P.dma(e7[:], e7_h[:])
    e15 = P.sb("e15", [64, 64]); P.dma(e15[:], e15_h[:])
    e13 = P.sb("e13", [64, 64]); P.dma(e13[:], e13_h[:])
    qlf = P.sb("qlf", [128, KT8 * 64]); 
    klf = P.sb("klf", [128, KT8 * 64])
    for ft in range(KT8):
        P.dma(qlf[:, ft * 64:(ft + 1) * 64], ql_h[ft * 128:(ft + 1) * 128, :])
        P.dma(klf[:, ft * 64:(ft + 1) * 64], kl_h[ft * 128:(ft + 1) * 128, :])
    Af = P.sb("Af", [64, H * 64])

    qlh0 = P.sb("qlh0", [64, 64])
    klh0 = P.sb("klh0", [64, 64])
    k2sb = P.sb("k2sb", [64, 64])
    k2T = P.sb("k2T", [64, 64])
    KVsb = P.sb("KVsb", [64, 64])
    KVT = P.sb("KVT", [64, 64])
    T1 = P.sb("T1", [64, 64])
    T3 = P.sb("T3", [64, 64])
    T4 = P.sb("T4", [64, 64])
    Vsb = P.sb("Vsb", [64, 64])
    VT = P.sb("VT", [64, 64])
    Bn = P.sb("Bn", [64, 64])
    ksb3 = P.sb("ksb3", [64, TC])
    p3 = P.sb("p3", [64, TC])
    pT = P.sb("pT", [128, 64])
    vh = P.sb("vh", [128, 64])
    runmax = P.sb("runmax", [64, 1])
    sumcol = P.sb("sumcol", [64, 1])
    negm = P.sb("negm", [64, 1])
    cA = P.sb("cA", [64, 1])
    d0 = P.sb("d0", [1, 1])
    psmall = P.ps("psmall", [128, 128])
    psB = P.ps("psB", [64, 64])

    def sm_free(buf, n, colmax, colsum):
        P.red(colmax[:, 0:1], buf, ALU.max)
        P.ts(colmax[:, 0:1], colmax[:, 0:1], -1.0, None, ALU.mult)
        P.op("scalar", lambda e: e.activation(buf, buf, AF.Exp, bias=colmax[:, 0:1],
                                              scale=1.0))
        P.red(colsum[:, 0:1], buf, ALU.add)
        P.recip(colsum[:, 0:1], colsum[:, 0:1])
        P.op("vector", lambda e: e.tensor_scalar(buf, buf, colsum[:, 0:1], None,
                                                 ALU.mult, ALU.bypass))

    for h in range(H):
        po, hf = 64 * (h % 2), h // 2
        P.dma(qlh0[:], qlf[po:po + 64, hf * 64:(hf + 1) * 64])
        P.dma(klh0[:], klf[po:po + 64, hf * 64:(hf + 1) * 64])
        P.mm(psmall[0:64, 0:64], qlh0[:], klh0[:])
        P.vcopy(k2sb[:], psmall[0:64, 0:64])
        sm_free(k2sb[:], 64, runmax, sumcol)
        P.mm(ctx.psr[0][:, 0:64], ctx.ones[0:64, 0:1], k2sb[:])
        P.red(d0[:, 0:1], ctx.psr[0][0:1, 0:64], ALU.max)
        P.recip(d0[:, 0:1], d0[:, 0:1])
        P.trans(psmall[0:64, 0:64], k2sb[:], ident[0:64, 0:64])
        P.vcopy(k2T[:], psmall[0:64, 0:64])
        P.mm(psmall[0:64, 0:1], ctx.ones[0:1, 0:64], d0[0:1, 0:1])
        P.vcopy(cA[:, 0:1], psmall[0:64, 0:1])
        P.ts(Vsb[:], k2T[:], cA[:, 0:1], None, ALU.mult)
        for _ in range(6):
            P.mm(psmall[0:64, 0:64], k2T[:], Vsb[:])
            P.vcopy(KVsb[:], psmall[0:64, 0:64])
            P.trans(psmall[0:64, 0:64], KVsb[:], ident[0:64, 0:64])
            P.vcopy(KVT[:], psmall[0:64, 0:64])
            P.tt(T1[:], e7[:], KVsb[:], ALU.subtract)
            P.mm(psmall[0:64, 0:64], KVT[:], T1[:])
            P.tt(T3[:], e15[:], psmall[0:64, 0:64], ALU.subtract)
            P.mm(psmall[0:64, 0:64], KVT[:], T3[:])
            P.tt(T4[:], e13[:], psmall[0:64, 0:64], ALU.subtract)
            P.trans(psmall[0:64, 0:64], Vsb[:], ident[0:64, 0:64])
            P.vcopy(VT[:], psmall[0:64, 0:64])
            P.mm(psmall[0:64, 0:64], VT[:], T4[:])
            P.ts(Vsb[:], psmall[0:64, 0:64], 0.25, None, ALU.mult)
        P.memset(runmax[:, 0:1], -3e38)
        for kc in range(NKC):
            P.dma(ksb3[:], k_h[h * 64:(h + 1) * 64, kc * TC:(kc + 1) * TC])
            P.mm(ctx.psb[0][0:64, 0:TC], qlh0[:], ksb3[:])
            P.red(cA[:, 0:1], ctx.psb[0][0:64, 0:TC], ALU.max)
            P.tt(runmax[:, 0:1], runmax[:, 0:1], cA[:, 0:1], ALU.max)
        P.ts(negm[:, 0:1], runmax[:, 0:1], -1.0, None, ALU.mult)
        P.memset(sumcol[:, 0:1], 0.0)
        for kc in range(NKC):
            P.dma(ksb3[:], k_h[h * 64:(h + 1) * 64, kc * TC:(kc + 1) * TC])
            P.mm(ctx.psb[0][0:64, 0:TC], qlh0[:], ksb3[:])
            P.op("scalar", lambda e: e.activation(p3[:], ctx.psb[0][0:64, 0:TC],
                                                  AF.Exp, bias=negm[:, 0:1], scale=1.0))
            P.red(cA[:, 0:1], p3[:], ALU.add)
            P.tt(sumcol[:, 0:1], sumcol[:, 0:1], cA[:, 0:1], ALU.add)
            for t4 in range(4):
                P.trans(psmall[0:128, 0:64], p3[0:64, t4 * 128:(t4 + 1) * 128], ident[0:64, 0:64])
                P.vcopy(pT[:], psmall[0:128, 0:64])
                P.dma(vh[:], v_h[kc * TC + t4 * 128: kc * TC + (t4 + 1) * 128,
                                 h * 64:(h + 1) * 64])
                P.mm(psB[:, 0:64], pT[:], vh[:],
                     start=(kc == 0 and t4 == 0), stop=(kc == NKC - 1 and t4 == 3))
        P.recip(cA[:, 0:1], sumcol[:, 0:1])
        P.ts(Bn[:], psB[:, 0:64], cA[:, 0:1], None, ALU.mult)
        P.trans(psmall[0:64, 0:64], Vsb[:], ident[0:64, 0:64])
        P.vcopy(VT[:], psmall[0:64, 0:64])
        P.mm(psmall[0:64, 0:64], VT[:], Bn[:])
        P.vcopy(Af[:, h * 64:(h + 1) * 64], psmall[0:64, 0:64])

    qfsb = P.sb("qfsb", [128, KT8 * NT])
    for ft in range(KT8):
        P.dma(qfsb[:, ft * NT:(ft + 1) * NT], q_h[ft * 128:(ft + 1) * 128, :])
    svf = P.sb("svf", [128, KT8 * NT])
    qh0 = P.sb("qh0", [64, 128])
    k1sb = P.sb("k1sb", [128, 64])
    k1T = P.sb("k1T", [64, 128])
    cB = P.sb("cB", [128, 1])
    cC = P.sb("cC", [128, 1])
    for t in range(NT // 128):
        for h in range(H):
            po, hf = 64 * (h % 2), h // 2
            P.dma(qh0[:], qfsb[po:po + 64, hf * NT + t * 128: hf * NT + (t + 1) * 128])
            P.dma(klh0[:], klf[po:po + 64, hf * 64:(hf + 1) * 64])
            P.mm(psmall[0:128, 0:64], qh0[:], klh0[:])
            P.vcopy(k1sb[:], psmall[0:128, 0:64])
            sm_free(k1sb[:], 64, cB, cC)
            P.trans(psmall[0:64, 0:128], k1sb[:], ident[0:128, 0:128])
            P.vcopy(k1T[:], psmall[0:64, 0:128])
            P.mm(psmall[po:po + 64, 0:128], Af[:, h * 64:(h + 1) * 64], k1T[:])
            P.vcopy(svf[po:po + 64, hf * NT + t * 128: hf * NT + (t + 1) * 128],
                    psmall[po:po + 64, 0:128])
    wp = wload(P, ctx, "w_p", wp_h, E, E)
    svc = P.sb("svc", [128, KT8 * TC], BF16)
    for ci in range(NCH):
        for ft in range(KT8):
            P.vcopy(svc[:, ft * TC:(ft + 1) * TC],
                    svf[:, ft * NT + ci * TC: ft * NT + (ci + 1) * TC])

        def ev(nt, ps, ci=ci):
            P.vcopy(qfsb[:, nt * NT + ci * TC: nt * NT + (ci + 1) * TC], ps)

        mm_chain(P, ctx, svc, TC, wp, E, E, 0, TC, ev)
    residual_ln_store(P, ctx, qfsb, NT, x_h, out_h)
    return P.build()


def b_l3f():
    P = Prog()
    ctx = Ctx(P, stage_w=2048)
    x_h = P.din("x", [E, NT])
    r1_h = P.dout("r1", [E, NT])
    rw_h = P.din("rW", [2, E, 8])
    rb_h = P.din("rb", [2, 8, 1])
    gb_h = P.din("gbias", [2, 1, 8])
    wup_h = P.din("eWup", [6, E, E])
    bup_h = P.din("ebupr", [6, 128, 8])
    wg_h = P.din("eWg", [6, E, E])
    bg_h = P.din("ebgr", [6, 128, 8])
    wd_h = P.din("eWd", [6, E, E])
    bd_h = P.din("ebdr", [6, 128, 8])
    sup_h = P.din("sWup", [E, 2048])
    sbup_h = P.din("sbupr", [2, 128, 8])
    swg_h = P.din("sWg", [E, E])
    sbg_h = P.din("sbgr", [128, 8])
    swd_h = P.din("sWd", [E, E])
    sbd_h = P.din("sbdr", [128, 8])
    nw_h = P.din("normw", [2, 128, 8])
    ident_h = P.din("ident", [128, 128])
    out_h = P.dout("xo", [E, NT])

    ident = P.sb("identsb", [128, 128]); P.dma(ident[:], ident_h[:])
    xfb = P.sb("xfb", [128, KT8 * NT], BF16)
    acc = P.sb("acc", [128, KT8 * NT])
    wup = P.sb("wup", [128, KT8 * E], BF16)
    wgg = P.sb("wgg", [128, KT8 * E], BF16)
    wdd = P.sb("wdd", [128, KT8 * E], BF16)
    hb = P.sb("hb", [128, KT8 * TC], BF16)
    g2b = P.sb("g2b", [128, KT8 * TC], BF16)
    bupc = P.sb("bupc", [128, 8])
    bgc = P.sb("bgc", [128, 8])
    bdc = P.sb("bdc", [128, 8])
    nwc = P.sb("nwc", [128, 8])
    w_tm = P.sb("w_tm", [128, (NT // 128) * 8])
    lgT = P.sb("lgT", [8, TC])
    lg_tm = P.sb("lg_tm", [128, 8])
    bi_tm = P.sb("bi_tm", [128, 8])
    sel_tm = P.sb("sel_tm", [128, 8])
    gbbc = P.sb("gbbc", [128, 8])
    growb = P.sb("growb", [1, TC])
    sb16 = P.sb("sb16", [128, 16 * TC], BF16)

    def one_round(j, in_hbm, rstore):
        for ft in range(KT8):
            P.dma(ctx.stage[:, 0:NT], in_hbm[ft * 128:(ft + 1) * 128, :])
            P.vcopy(xfb[:, ft * NT:(ft + 1) * NT], ctx.stage[:, 0:NT])
        P.memset(acc[:], 0.0)
        wr = wload(P, ctx, f"w_r{j}", rw_h[j], E, 8)
        rbc = P.sb(f"rbc{j}", [8, 1]); P.dma(rbc[:], rb_h[j])
        P.dma(growb[0:1, 0:8], gb_h[j])
        P.mm(ctx.psb[3][0:128, 0:8], ctx.ones[0:1, :], growb[0:1, 0:8])
        P.vcopy(gbbc[:], ctx.psb[3][0:128, 0:8])
        for ci in range(NCH):
            ps = ctx.psb[0]
            for kt in range(KT8):
                P.mm(ps[0:8, 0:TC], wr[:, kt * 8:(kt + 1) * 8],
                     xfb[:, kt * NT + ci * TC: kt * NT + (ci + 1) * TC],
                     start=(kt == 0), stop=(kt == KT8 - 1))
            P.ts(lgT[:, 0:TC], ps[0:8, 0:TC], rbc[:, 0:1], None, ALU.add)
            for t4 in range(4):
                t = ci * 4 + t4
                P.trans(ctx.psb[3][0:128, 0:8], lgT[0:8, t4 * 128:(t4 + 1) * 128], ident[0:8, 0:8])
                P.vcopy(lg_tm[:], ctx.psb[3][0:128, 0:8])
                P.ts(lg_tm[:, 6:8], lg_tm[:, 6:8], 0.0, -1e30, ALU.mult, ALU.add)
                P.tt(bi_tm[:], lg_tm[:], gbbc[:], ALU.add)
                P.memset(sel_tm[:], 0.0)
                topk_iter2(P, ctx, bi_tm[:], sel_tm[:], 2)
                P.stt(bi_tm[:], sel_tm[:], 1e30, lg_tm[:], ALU.mult, ALU.add)
                P.ts(bi_tm[:], bi_tm[:], -1e30, None, ALU.add)
                P.red(ctx.cols[0][:, 0:1], bi_tm[:], ALU.max)
                P.ts(ctx.cols[0][:, 0:1], ctx.cols[0][:, 0:1], -1.0, None, ALU.mult)
                P.op("scalar", lambda e: e.activation(bi_tm[:], bi_tm[:], AF.Exp,
                                                      bias=ctx.cols[0][:, 0:1], scale=1.0))
                P.red(ctx.cols[1][:, 0:1], bi_tm[:], ALU.add)
                P.recip(ctx.cols[1][:, 0:1], ctx.cols[1][:, 0:1])
                P.ts(w_tm[:, t * 8:(t + 1) * 8], bi_tm[:], ctx.cols[1][:, 0:1],
                     None, ALU.mult)
        for e in range(6):
            wfill(P, ctx, wup, wup_h[e], E, E)
            wfill(P, ctx, wgg, wg_h[e], E, E)
            wfill(P, ctx, wdd, wd_h[e], E, E)
            P.dma(bupc[:], bup_h[e])
            P.dma(bgc[:], bg_h[e])
            P.dma(bdc[:], bd_h[e])
            for ci in range(NCH):
                for nt in range(KT8):
                    for kt in range(KT8):
                        P.mm(ctx.psb[0][:, 0:TC],
                             wup[:, kt * E + nt * 128: kt * E + nt * 128 + 128],
                             xfb[:, kt * NT + ci * TC: kt * NT + (ci + 1) * TC],
                             start=(kt == 0), stop=(kt == KT8 - 1))
                    P.ts(hb[:, nt * TC:(nt + 1) * TC], ctx.psb[0][:, 0:TC],
                         bupc[:, nt:nt + 1], None, ALU.add)
                for nt in range(KT8):
                    for kt in range(KT8):
                        P.mm(ctx.psb[0][:, 0:TC],
                             wgg[:, kt * E + nt * 128: kt * E + nt * 128 + 128],
                             hb[:, kt * TC:(kt + 1) * TC],
                             start=(kt == 0), stop=(kt == KT8 - 1))
                    P.op("scalar", lambda e2, nt=nt: e2.activation(
                        g2b[:, nt * TC:(nt + 1) * TC], ctx.psb[0][:, 0:TC], AF.Silu,
                        bias=bgc[:, nt:nt + 1], scale=1.0))
                P.tt(hb[:], g2b[:], hb[:], ALU.mult)
                bc = gate_row_bc(P, ctx, w_tm, 8, e, ci * TC, TC, ident,
                                 ctx.psb[3], growb)
                for dt in range(KT8):
                    for kt in range(KT8):
                        P.mm(ctx.psb[2][:, 0:TC],
                             wdd[:, kt * E + dt * 128: kt * E + dt * 128 + 128],
                             hb[:, kt * TC:(kt + 1) * TC],
                             start=(kt == 0), stop=(kt == KT8 - 1))
                    P.ts(ctx.tmps[0][:, 0:TC], ctx.psb[2][:, 0:TC],
                         bdc[:, dt:dt + 1], None, ALU.add)
                    P.tt(ctx.tmps[0][:, 0:TC], ctx.tmps[0][:, 0:TC], bc, ALU.mult)
                    a = acc[:, dt * NT + ci * TC: dt * NT + (ci + 1) * TC]
                    P.tt(a, a, ctx.tmps[0][:, 0:TC], ALU.add)
        for ns in range(2):
            wfill(P, ctx, wup, sup_h[:, ns * E:(ns + 1) * E], E, E)
            P.dma(bupc[:], sbup_h[ns])
            if ns == 0:
                wfill(P, ctx, wgg, swg_h, E, E)
                wfill(P, ctx, wdd, swd_h, E, E)
                P.dma(bgc[:], sbg_h[:])
                P.dma(bdc[:], sbd_h[:])
            for ci in range(NCH):
                for nt in range(KT8):
                    for kt in range(KT8):
                        P.mm(ctx.psb[0][:, 0:TC],
                             wup[:, kt * E + nt * 128: kt * E + nt * 128 + 128],
                             xfb[:, kt * NT + ci * TC: kt * NT + (ci + 1) * TC],
                             start=(kt == 0), stop=(kt == KT8 - 1))
                    P.ts(hb[:, nt * TC:(nt + 1) * TC], ctx.psb[0][:, 0:TC],
                         bupc[:, nt:nt + 1], None, ALU.add)
                for nt in range(KT8):
                    for kt in range(KT8):
                        P.mm(ctx.psb[0][:, 0:TC],
                             wgg[:, kt * E + nt * 128: kt * E + nt * 128 + 128],
                             hb[:, kt * TC:(kt + 1) * TC],
                             start=(kt == 0), stop=(kt == KT8 - 1))
                    P.op("scalar", lambda e2, nt=nt: e2.activation(
                        g2b[:, nt * TC:(nt + 1) * TC], ctx.psb[0][:, 0:TC], AF.Silu,
                        bias=bgc[:, nt:nt + 1], scale=1.0))
                P.tt(hb[:], g2b[:], hb[:], ALU.mult)
                for dt in range(KT8):
                    for kt in range(KT8):
                        P.mm(ctx.psb[2][:, 0:TC],
                             wdd[:, kt * E + dt * 128: kt * E + dt * 128 + 128],
                             hb[:, kt * TC:(kt + 1) * TC],
                             start=(kt == 0), stop=(kt == KT8 - 1))
                    P.ts(ctx.tmps[0][:, 0:TC], ctx.psb[2][:, 0:TC],
                         bdc[:, dt:dt + 1], None, ALU.add)
                    a = acc[:, dt * NT + ci * TC: dt * NT + (ci + 1) * TC]
                    P.tt(a, a, ctx.tmps[0][:, 0:TC], ALU.add)
        P.dma(nwc[:], nw_h[j])
        for ci in range(NCH):
            for ft in range(KT8):
                P.dma(ctx.stage[:, 0:TC], in_hbm[ft * 128:(ft + 1) * 128,
                                                 ci * TC:(ci + 1) * TC])
                a = acc[:, ft * NT + ci * TC: ft * NT + (ci + 1) * TC]
                P.tt(a, a, ctx.stage[:, 0:TC], ALU.add)

            def store_r(ft, ap, ci=ci):
                if rstore is not None:
                    P.dma(rstore[ft * 128:(ft + 1) * 128, ci * TC:(ci + 1) * TC], ap)

            rms_chunk(P, ctx, acc, NT, ci * TC, TC, 1e-6, store_r, gcol=nwc)

    one_round(0, x_h, r1_h)
    one_round(1, r1_h, None)
    residual_ln_store(P, ctx, acc, NT, x_h, out_h)
    return P.build()


def _cc(a):
    return np.ascontiguousarray(a, dtype=np.float32)


def _cols(b):
    n = b.shape[-1]
    return _cc(b.reshape(n // 128, 128).T)


def _consts():
    a = 1.0 / 10000.0 ** np.linspace(0.0, 1.0, 32)
    angle = np.repeat(a, 2)
    decay = np.log(1.0 - 2.0 ** (-5.0 - np.arange(16.0)))
    return angle, decay


def _retnet_tables(c):
    angle, decay = _consts()
    offb = (c % 4) * NT
    t = offb + np.arange(NT)
    cos2 = np.cos(t[None, :] * angle[:, None])
    sin2 = np.sin(t[None, :] * angle[:, None])
    cos2 = _cc(np.tile(cos2, (2, 1)))
    sin2 = _cc(np.tile(sin2, (2, 1)))
    f = np.arange(E)
    hh = f // 64
    ed = np.exp(decay)
    qpos = t
    e_d = decay[hh][:, None]
    s = (1.0 - np.exp(e_d * (qpos[None, :] + 1.0))) / (1.0 - np.exp(e_d))
    facQ = _cc(np.exp(e_d * (qpos[None, :] % TC)) / np.sqrt(s))
    j = np.arange(TC)
    facK = _cc(np.exp(-decay[hh][:, None] * j[None, :]))
    col = np.zeros((128, H * 64), np.float64)
    for h in range(H):
        for qc in range(NCH):
            q0 = offb + qc * TC
            for kc in range(NKC):
                k0 = kc * TC
                col[:, h * 64 + qc * 16 + kc] = np.exp(decay[h] * (q0 - k0)) if k0 <= q0 else 0.0
    sgn = np.where(np.arange(128) % 2 == 0, -1.0, 1.0).reshape(128, 1)
    return cos2, sin2, _cc(col), _cc(sgn)


def _qk_idx(c, causal):
    offb = (c % 4) * NT
    if causal:
        qidx = _cc((offb + np.arange(NT)).reshape(1, NT))
    else:
        qidx = _cc(np.full((1, NT), 1e9))
    kidx = np.zeros((128, NKC * 4), np.float32)
    for kc in range(NKC):
        for kt in range(4):
            kidx[:, kc * 4 + kt] = kc * TC + kt * 128 + np.arange(128)
    return qidx, _cc(kidx)


_PROGS = {}


def _prog(key, builder):
    if key not in _PROGS:
        _PROGS[key] = builder()
    return _PROGS[key]


def _launch(nc, in_maps):
    res = run_bass_kernel_spmd(nc, in_maps, core_ids=list(range(NCORES)))
    return res.results


def _gfm(res, key):
    return np.concatenate([np.asarray(res[c][key]) for c in range(NCORES)], axis=1)


def _gtm(res, key):
    return np.concatenate([np.asarray(res[c][key]) for c in range(NCORES)], axis=0)


def _slc(act, c):
    return _cc(act[:, c * NT:(c + 1) * NT])


def _fmb(act, c):
    b = c // 4
    return _cc(act[:, b * S:(b + 1) * S])


def _tmb(act, c):
    b = c // 4
    return _cc(act[b * S:(b + 1) * S, :])


IDENT = np.eye(128, dtype=np.float32)


def run_chain(inp, dump=None):
    x0 = np.asarray(inp["L0_attn_x"], np.float32)
    xf = _cc(x0.reshape(B * S, E).T)

    def dmp(name, val):
        if dump is not None:
            dump[name] = val

    perm = np.arange(E); perm = perm + np.where(perm % 2 == 0, 1, -1)

    nc = _prog("l0a_proj", b_l0a_proj)
    ims = []
    for c in range(NCORES):
        cos2, sin2, colfac, sgn = _retnet_tables(c)
        ims.append({"x": _slc(xf, c),
                    "Wq": _cc(inp["L0_attn_Wq"]), "Wk": _cc(inp["L0_attn_Wk"]),
                    "Wv": _cc(inp["L0_attn_Wv"]), "Wg": _cc(inp["L0_attn_Wg"]),
                    "WqP": _cc(np.asarray(inp["L0_attn_Wq"])[:, perm]),
                    "WkP": _cc(np.asarray(inp["L0_attn_Wk"])[:, perm]),
                    "cos2": cos2, "sin2": sin2, "sgncol": sgn})
    res = _launch(nc, ims)
    qr = _gfm(res, "qr"); kr = _gfm(res, "kr"); vt = _gtm(res, "v"); gg = _gfm(res, "g")
    dmp("l0_qr", qr)

    nc = _prog("l0a_attn", b_l0a_attn)
    ims = []
    for c in range(NCORES):
        cos2, sin2, colfac, sgn = _retnet_tables(c)
        angle, decay = _consts()
        offb = (c % 4) * NT
        tt = offb + np.arange(NT)
        f = np.arange(E); hh = f // 64
        e_d = decay[hh][:, None]
        s = (1.0 - np.exp(e_d * (tt[None, :] + 1.0))) / (1.0 - np.exp(e_d))
        facQ = _cc(np.exp(e_d * (tt[None, :] % TC)) / np.sqrt(s))
        facK = _cc(np.exp(-decay[hh][:, None] * np.arange(TC)[None, :]))
        qidx, kidx = _qk_idx(c, True)
        ims.append({"qr": _slc(qr, c), "krF": _fmb(kr, c), "vF": _tmb(vt, c),
                    "g": _slc(gg, c), "x": _slc(xf, c), "facQ": facQ, "facK": facK,
                    "colfac": colfac, "qidx": qidx, "kidx": kidx})
    res = _launch(nc, ims)
    x1 = _gfm(res, "xo")
    dmp("x1", x1)

    nc = _prog("l0f", b_l0f)
    G = np.zeros((32, 8), np.float32)
    for e in range(32):
        G[e, e // 4] = 1.0
    egT = _cc(np.asarray(inp["L0_ffn_expert_gate"]).transpose(0, 2, 1))
    euT = _cc(np.asarray(inp["L0_ffn_expert_up"]).transpose(0, 2, 1))
    edT = _cc(np.asarray(inp["L0_ffn_expert_down"]).transpose(0, 2, 1))
    ims = [{"x": _slc(x1, c), "gwT": _cc(np.asarray(inp["L0_ffn_gate_w"]).T),
            "egT": egT, "euT": euT, "edT": edT,
            "sgT": _cc(np.asarray(inp["L0_ffn_shared_gate_w"]).T),
            "suT": _cc(np.asarray(inp["L0_ffn_shared_up_w"]).T),
            "sdT": _cc(np.asarray(inp["L0_ffn_shared_down_w"]).T),
            "G": G, "G2": _cc(G.T), "ident": IDENT} for c in range(NCORES)]
    res = _launch(nc, ims)
    x2 = _gfm(res, "xo")
    dmp("x2", x2)

    nc = _prog("qkv_b", lambda: b_qkv_proj(True))
    ims = [{"x": _slc(x2, c), "Wq": _cc(np.asarray(inp["L1_attn_Wq"]).T),
            "Wk": _cc(np.asarray(inp["L1_attn_Wk"]).T),
            "Wv": _cc(np.asarray(inp["L1_attn_Wv"]).T),
            "bq": _cols(np.asarray(inp["L1_attn_bq"])),
            "bk": _cols(np.asarray(inp["L1_attn_bk"])),
            "bv": _cols(np.asarray(inp["L1_attn_bv"])),
            "bvrow": _cc(np.asarray(inp["L1_attn_bv"]).reshape(1, E))}
           for c in range(NCORES)]
    res = _launch(nc, ims)
    q1 = _gfm(res, "q"); k1 = _gfm(res, "k"); v1 = _gtm(res, "v")

    nc = _prog("soft_plain", lambda: b_soft_attn(False, "Wo", False))
    ims = []
    for c in range(NCORES):
        qidx, kidx = _qk_idx(c, True)
        ims.append({"q": _slc(q1, c), "kF": _fmb(k1, c), "vF": _tmb(v1, c),
                    "x": _slc(x2, c), "Wo": _cc(np.asarray(inp["L1_attn_Wo"]).T),
                    "bo": _cols(np.asarray(inp["L1_attn_bo"])),
                    "qidx": qidx, "kidx": kidx})
    res = _launch(nc, ims)
    x3 = _gfm(res, "xo")
    dmp("x3", x3)

    nc = _prog("qkv_nb", lambda: b_qkv_proj(False))
    ims = [{"x": _slc(x3, c), "Wq": _cc(inp["L1_ffn_Wq"]), "Wk": _cc(inp["L1_ffn_Wk"]),
            "Wv": _cc(inp["L1_ffn_Wv"])} for c in range(NCORES)]
    res = _launch(nc, ims)
    qf = _gfm(res, "q"); kf = _gfm(res, "k"); vf = _gtm(res, "v")

    nc = _prog("soft_ln", lambda: b_soft_attn(False, "Wo", True))
    ims = []
    for c in range(NCORES):
        qidx, kidx = _qk_idx(c, False)
        ims.append({"q": _slc(qf, c), "kF": _fmb(kf, c), "vF": _tmb(vf, c),
                    "x": _slc(x3, c), "Wo": _cc(inp["L1_ffn_Wo"]),
                    "bo": _cols(np.asarray(inp["L1_ffn_bo"])),
                    "lng": _cols(np.asarray(inp["L1_ffn_ln1_g"])),
                    "lnb": _cols(np.asarray(inp["L1_ffn_ln1_b"])),
                    "qidx": qidx, "kidx": kidx})
    res = _launch(nc, ims)
    out1 = _gfm(res, "xo")
    dmp("out1", out1)

    nc = _prog("l1f_moe", b_l1f_moe)
    be1r = np.stack([_cols(np.asarray(inp["L1_ffn_be1"])[e]) for e in range(8)])
    be2r = np.stack([_cols(np.asarray(inp["L1_ffn_be2"])[e]) for e in range(8)])
    ims = [{"out1": _slc(out1, c), "x3": _slc(x3, c),
            "Wr": np.pad(_cc(inp["L1_ffn_Wr"]), ((0, 0), (0, 0))),
            "We1": _cc(inp["L1_ffn_We1"]), "be1r": _cc(be1r),
            "We2": _cc(inp["L1_ffn_We2"]), "be2r": _cc(be2r),
            "ln2g": _cols(np.asarray(inp["L1_ffn_ln2_g"])),
            "ln2b": _cols(np.asarray(inp["L1_ffn_ln2_b"])),
            "ident": IDENT} for c in range(NCORES)]
    res = _launch(nc, ims)
    x4 = _gfm(res, "xo")
    dmp("x4", x4)

    nc = _prog("qkv_nb", lambda: b_qkv_proj(False))
    ims = [{"x": _slc(x4, c), "Wq": _cc(np.asarray(inp["L2_attn_Wq"]).reshape(E, E)),
            "Wk": _cc(np.asarray(inp["L2_attn_Wk"]).reshape(E, E)),
            "Wv": _cc(np.asarray(inp["L2_attn_Wv"]).reshape(E, E))}
           for c in range(NCORES)]
    res = _launch(nc, ims)
    q2 = _gfm(res, "q"); k2 = _gfm(res, "k"); v2 = _gtm(res, "v")

    nc = _prog("soft_plain", lambda: b_soft_attn(False, "Wo", False))
    ims = []
    for c in range(NCORES):
        qidx, kidx = _qk_idx(c, True)
        ims.append({"q": _slc(q2, c), "kF": _fmb(k2, c), "vF": _tmb(v2, c),
                    "x": _slc(x4, c),
                    "Wo": _cc(np.asarray(inp["L2_attn_Wout"]).reshape(E, E)),
                    "bo": np.zeros((128, KT8), np.float32),
                    "qidx": qidx, "kidx": kidx})
    res = _launch(nc, ims)
    x5 = _gfm(res, "xo")
    dmp("x5", x5)

    nc = _prog("l2f_a", b_l2f_a)
    ims = [{"x": _slc(x5, c), "lng": _cols(np.asarray(inp["L2_ffn_ln_gamma"])),
            "lnb": _cols(np.asarray(inp["L2_ffn_ln_beta"])),
            "fc1q": np.ascontiguousarray(inp["L2_ffn_fc1_q"], np.int32),
            "fc1s": _cc(inp["L2_ffn_fc1_scales"]),
            "b1r": _cols(np.asarray(inp["L2_ffn_fc1_bias"]))} for c in range(NCORES)]
    res = _launch(nc, ims)
    ys = [np.asarray(res[c]["y"]) for c in range(NCORES)]

    nc = _prog("l2f_b", b_l2f_b)
    ims = [{"x": _slc(x5, c), "y": ys[c],
            "fc2q": np.ascontiguousarray(inp["L2_ffn_fc2_q"], np.int32),
            "fc2s": _cc(inp["L2_ffn_fc2_scales"]),
            "b2r": _cols(np.asarray(inp["L2_ffn_fc2_bias"]))} for c in range(NCORES)]
    res = _launch(nc, ims)
    x6 = _gfm(res, "xo")
    dmp("x6", x6)

    nc = _prog("l3a_proj", b_l3a_proj)
    ims = [{"x": _slc(x6, c), "wqkv": _cc(inp["L3_attn_w_qkv"])} for c in range(NCORES)]
    res = _launch(nc, ims)
    q3 = _gfm(res, "q"); k3 = _gfm(res, "k"); v3 = _gtm(res, "v")
    qlp = [np.asarray(res[c]["qlp"]) for c in range(NCORES)]
    klp = [np.asarray(res[c]["klp"]) for c in range(NCORES)]
    Qls = [np.concatenate(qlp[4 * b:4 * b + 4], axis=1) for b in range(2)]
    Kls = [np.concatenate(klp[4 * b:4 * b + 4], axis=1) for b in range(2)]

    nc = _prog("l3a_core", b_l3a_core)
    ims = []
    for c in range(NCORES):
        b = c // 4
        ims.append({"q": _slc(q3, c), "kF": _fmb(k3, c), "vF": _tmb(v3, c),
                    "Ql": _cc(Qls[b]), "Kl": _cc(Kls[b]), "x": _slc(x6, c),
                    "wproj": _cc(inp["L3_attn_w_proj"]), "ident": IDENT,
                    "eye7": _cc(7 * np.eye(64)), "eye15": _cc(15 * np.eye(64)),
                    "eye13": _cc(13 * np.eye(64))})
    res = _launch(nc, ims)
    x7 = _gfm(res, "xo")
    dmp("x7", x7)

    nc = _prog("l3f", b_l3f)
    rW = np.zeros((2, E, 8), np.float32); rW[:, :, :6] = np.asarray(inp["L3_ffn_router_W"])
    rb = np.zeros((2, 8, 1), np.float32); rb[:, :6, 0] = np.asarray(inp["L3_ffn_router_b"])
    gb = np.zeros((2, 1, 8), np.float32); gb[:, 0, :6] = np.asarray(inp["L3_ffn_gate_bias"])
    ebupr = np.stack([_cols(np.asarray(inp["L3_ffn_exp_bup"])[e]) for e in range(6)])
    ebgr = np.stack([_cols(np.asarray(inp["L3_ffn_exp_bg"])[e]) for e in range(6)])
    ebdr = np.stack([_cols(np.asarray(inp["L3_ffn_exp_bdown"])[e]) for e in range(6)])
    sbupr = np.stack([_cols(np.asarray(inp["L3_ffn_share_bup"])[ns * 1024:(ns + 1) * 1024])
                      for ns in range(2)])
    nwr = np.stack([_cols(np.asarray(inp["L3_ffn_norm_w"])[j]) for j in range(2)])
    ims = [{"x": _slc(x7, c), "rW": rW, "rb": rb, "gbias": gb,
            "eWup": _cc(inp["L3_ffn_exp_Wup"]), "ebupr": _cc(ebupr),
            "eWg": _cc(inp["L3_ffn_exp_Wg"]), "ebgr": _cc(ebgr),
            "eWd": _cc(inp["L3_ffn_exp_Wdown"]), "ebdr": _cc(ebdr),
            "sWup": _cc(inp["L3_ffn_share_Wup"]), "sbupr": _cc(sbupr),
            "sWg": _cc(inp["L3_ffn_share_Wg"]),
            "sbgr": _cols(np.asarray(inp["L3_ffn_share_bg"])),
            "sWd": _cc(inp["L3_ffn_share_Wdown"]),
            "sbdr": _cols(np.asarray(inp["L3_ffn_share_bdown"])),
            "normw": _cc(nwr), "ident": IDENT} for c in range(NCORES)]
    res = _launch(nc, ims)
    xo = _gfm(res, "xo")

    out = np.empty((B, S, E), np.float32)
    for c in range(NCORES):
        b, off = c // 4, (c % 4) * NT
        out[b, off:off + NT, :] = np.asarray(res[c]["xo"]).T
    return out


def kernel(**inputs):
    return run_chain(inputs)


# version 03 (final)
# speedup vs baseline: 1.3242x; 1.3242x over previous
import numpy as np
import concourse.bass as bass
import concourse.mybir as mybir
from concourse.bass_utils import run_bass_kernel_spmd

F32 = mybir.dt.float32
BF16 = mybir.dt.bfloat16
I32 = mybir.dt.int32
AF = mybir.ActivationFunctionType
ALU = mybir.AluOpType
AX = mybir.AxisListType

ENGINES = ["sync", "scalar", "vector", "tensor", "gpsimd"]

B, S, E, H, D = 2, 8192, 1024, 16, 64
NT = 2048
TC = 512
NCH = NT // TC
KT8 = E // 128
NKC = S // TC
NCORES = 8


class Prog:
    def __init__(self):
        self.nc = bass.Bass("TRN2", target_bir_lowering=False)
        self.tape = []

    def din(self, name, shape, dtype=F32):
        return self.nc.dram_tensor(name, list(shape), dtype, kind="ExternalInput").ap()

    def dout(self, name, shape, dtype=F32):
        return self.nc.dram_tensor(name, list(shape), dtype, kind="ExternalOutput").ap()

    def sb(self, name, shape, dtype=F32):
        return self.nc.alloc_sbuf_tensor(name, list(shape), dtype)

    def ps(self, name, shape, dtype=F32):
        return self.nc.alloc_psum_tensor(name, list(shape), dtype)

    def op(self, engine, fn, inc=1, nowait=False):
        self.tape.append((engine, fn, inc, nowait))

    def dma(self, out, in_, q="sync", **kw):
        self.op(q, lambda e: e.dma_start(out=out, in_=in_, **kw), inc=16)

    def mm(self, out, lhsT, rhs, start=True, stop=True, **kw):
        self.op("tensor", lambda e: e.matmul(out, lhsT, rhs, start=start, stop=stop, **kw),
                nowait=(not start))

    def trans(self, out, in_, ident):
        self.op("tensor", lambda e: e.transpose(out, in_, ident))

    def act(self, out, in_, func, bias=0.0, scale=1.0):
        self.op("scalar", lambda e: e.activation(out, in_, func, bias=bias, scale=scale))

    def scopy(self, out, in_):
        self.op("scalar", lambda e: e.copy(out, in_))

    def vcopy(self, out, in_):
        self.op("vector", lambda e: e.tensor_copy(out, in_))

    def gcopy(self, out, in_):
        self.op("gpsimd", lambda e: e.tensor_copy(out, in_))

    def tt(self, out, a, b, alu):
        self.op("vector", lambda e: e.tensor_tensor(out, a, b, alu))

    def ts(self, out, in0, s1, s2=None, op0=ALU.mult, op1=ALU.bypass):
        self.op("vector", lambda e: e.tensor_scalar(out, in0, s1, s2, op0, op1))

    def stt(self, out, in0, scalar, in1, op0, op1):
        self.op("vector", lambda e: e.scalar_tensor_tensor(out, in0, scalar, in1, op0, op1))

    def red(self, out, in_, alu, axis=AX.X, absval=None):
        self.op("vector", lambda e: e.tensor_reduce(out, in_, axis, alu,
                                                    apply_absolute_value=absval))

    def recip(self, out, in_):
        self.op("vector", lambda e: e.reciprocal(out, in_))

    def memset(self, ap, v, eng="vector"):
        self.op(eng, lambda e: e.memset(ap, v))

    def build(self):
        nc = self.nc
        sem = nc.alloc_semaphore("gsem")
        cums = []
        c = 0
        for t in self.tape:
            cums.append(c)
            c += t[2]
        total = c
        used = [en for en in ENGINES if any(t[0] == en for t in self.tape)]
        with nc.Block() as block:
            for en in used:
                entries = [(i, t[1], t[2], t[3]) for i, t in enumerate(self.tape)
                           if t[0] == en]

                def body(eng, entries=entries):
                    for i, fn, inc, nowait in entries:
                        ok_skip = nowait and i > 0 and self.tape[i - 1][0] == "tensor"
                        if cums[i] > 0 and not ok_skip:
                            eng.wait_ge(sem, cums[i])
                        fn(eng).then_inc(sem, inc)
                    eng.wait_ge(sem, total)

                getattr(block, en)(body)
        return nc


class Ctx:

    def __init__(self, P, n_ps=4, stage_w=4096):
        self.P = P
        self.stage_w = stage_w
        self.stage = P.sb("stage", [128, stage_w])
        self.ones = P.sb("ones_c", [128, 128])
        P.memset(self.ones[:], 1.0)
        self.psb = [P.ps(f"psb{i}", [128, TC]) for i in range(n_ps)]
        self.psr = [P.ps(f"psr{i}", [1, TC]) for i in range(2)]
        self.rows = [P.sb(f"rowb{i}", [1, TC]) for i in range(3)]
        self.bcs = [P.sb(f"bcb{i}", [128, TC]) for i in range(2)]
        self.tmps = [P.sb(f"tmpb{i}", [128, TC]) for i in range(2)]
        self.cols = [P.sb(f"colb{i}", [128, 1]) for i in range(3)]


def wfill(P, ctx, wsb, hbm, K, N, scale=None):
    stage_w = ctx.stage_w
    for kt in range(K // 128):
        for c0 in range(0, N, stage_w):
            cw = min(stage_w, N - c0)
            P.dma(ctx.stage[:, 0:cw], hbm[kt * 128:(kt + 1) * 128, c0:c0 + cw])
            dst = wsb[:, kt * N + c0: kt * N + c0 + cw]
            if scale is None:
                P.vcopy(dst, ctx.stage[:, 0:cw])
            else:
                P.ts(dst, ctx.stage[:, 0:cw], scale, None, ALU.mult)


def wload(P, ctx, name, hbm, K, N, scale=None):
    wsb = P.sb(name, [128, (K // 128) * N], BF16)
    wfill(P, ctx, wsb, hbm, K, N, scale)
    return wsb


def aload(P, ctx, dst, dcol0, hbm, c0, cw, width):
    for ft in range(KT8):
        P.dma(ctx.stage[:, 0:cw], hbm[ft * 128:(ft + 1) * 128, c0:c0 + cw])
        P.vcopy(dst[:, ft * width + dcol0: ft * width + dcol0 + cw], ctx.stage[:, 0:cw])


def mm_chain(P, ctx, xsb, xw, wsb, K, N, tc0, tcw, evac, psi=0):
    KT = K // 128
    ps = ctx.psb[psi]
    for nt in range(N // 128):
        for kt in range(KT):
            P.mm(ps[:, 0:tcw], wsb[:, kt * N + nt * 128: kt * N + nt * 128 + 128],
                 xsb[:, kt * xw + tc0: kt * xw + tc0 + tcw],
                 start=(kt == 0), stop=(kt == KT - 1))
        evac(nt, ps[:, 0:tcw])


def row_bcast(P, ctx, row_ap, tcw, nrows=128, bci=0, pbase=0):
    ps = ctx.psb[3]
    P.mm(ps[pbase:pbase + nrows, 0:tcw], ctx.ones[0:1, 0:nrows], row_ap)
    bc = ctx.bcs[bci]
    P.vcopy(bc[pbase:pbase + nrows, 0:tcw], ps[pbase:pbase + nrows, 0:tcw])
    return bc[pbase:pbase + nrows, 0:tcw]


def psum_rows(P, ctx, src_tiles, tcw, pri=0, square=False, nparts=128):
    pr = ctx.psr[pri]
    n = len(src_tiles)
    for i, t in enumerate(src_tiles):
        rhs = t
        if square:
            P.act(ctx.tmps[1][0:nparts, 0:tcw], t, AF.Square)
            rhs = ctx.tmps[1][0:nparts, 0:tcw]
        P.mm(pr[:, 0:tcw], ctx.ones[0:nparts, 0:1], rhs, start=(i == 0), stop=(i == n - 1))
    return pr


def ln_chunk(P, ctx, ysb, yw, tc0, tcw, eps, out_cb, gcol=None, bcol=None):
    tiles = [ysb[:, ft * yw + tc0: ft * yw + tc0 + tcw] for ft in range(KT8)]
    prm = psum_rows(P, ctx, tiles, tcw, pri=0)
    prv = psum_rows(P, ctx, tiles, tcw, pri=1, square=True)
    mean = ctx.rows[0]
    P.ts(mean[:, 0:tcw], prm[:, 0:tcw], 1.0 / 1024.0, None, ALU.mult)
    var = ctx.rows[1]
    P.ts(var[:, 0:tcw], prv[:, 0:tcw], 1.0 / 1024.0, None, ALU.mult)
    m2 = ctx.rows[2]
    P.tt(m2[:, 0:tcw], mean[:, 0:tcw], mean[:, 0:tcw], ALU.mult)
    P.tt(var[:, 0:tcw], var[:, 0:tcw], m2[:, 0:tcw], ALU.subtract)
    P.ts(var[:, 0:tcw], var[:, 0:tcw], eps, None, ALU.add)
    P.act(var[:, 0:tcw], var[:, 0:tcw], AF.Sqrt)
    P.recip(var[:, 0:tcw], var[:, 0:tcw])
    bcm = row_bcast(P, ctx, mean[:, 0:tcw], tcw, bci=0)
    bcr = row_bcast(P, ctx, var[:, 0:tcw], tcw, bci=1)
    for ft in range(KT8):
        t = tiles[ft]
        P.tt(t, t, bcm, ALU.subtract)
        P.tt(t, t, bcr, ALU.mult)
        if gcol is not None:
            P.ts(t, t, gcol[:, ft:ft + 1], bcol[:, ft:ft + 1], ALU.mult, ALU.add)
        out_cb(ft, t)


def rms_chunk(P, ctx, ysb, yw, tc0, tcw, eps, out_cb, gcol=None):
    tiles = [ysb[:, ft * yw + tc0: ft * yw + tc0 + tcw] for ft in range(KT8)]
    prv = psum_rows(P, ctx, tiles, tcw, pri=1, square=True)
    var = ctx.rows[1]
    P.ts(var[:, 0:tcw], prv[:, 0:tcw], 1.0 / 1024.0, eps, ALU.mult, ALU.add)
    P.act(var[:, 0:tcw], var[:, 0:tcw], AF.Sqrt)
    P.recip(var[:, 0:tcw], var[:, 0:tcw])
    bcr = row_bcast(P, ctx, var[:, 0:tcw], tcw, bci=1)
    for ft in range(KT8):
        t = tiles[ft]
        P.tt(t, t, bcr, ALU.mult)
        if gcol is not None:
            P.ts(t, t, gcol[:, ft:ft + 1], None, ALU.mult)
        out_cb(ft, t)


def residual_ln_store(P, ctx, fsb, fw, x_hbm, out_hbm, eps=1e-5, ysb=None):
    for ci in range(NCH):
        tc0 = ci * TC
        for ft in range(KT8):
            P.dma(ctx.stage[:, 0:TC], x_hbm[ft * 128:(ft + 1) * 128, tc0:tc0 + TC])
            t = fsb[:, ft * fw + tc0: ft * fw + tc0 + TC]
            P.tt(t, t, ctx.stage[:, 0:TC], ALU.add)

        def store(ft, ap, tc0=tc0):
            P.dma(out_hbm[ft * 128:(ft + 1) * 128, tc0:tc0 + TC], ap)

        ln_chunk(P, ctx, fsb, fw, tc0, TC, eps, store)


def gate_row_bc(P, ctx, gate_tm, ncols, e, tc0, tcw, ident, pst, rowbuf):
    P = P
    for i in range(tcw // 128):
        t = (tc0 + i * 128) // 128
        P.trans(pst[0:1, 0:128], gate_tm[:, t * ncols + e: t * ncols + e + 1], ident[0:128, 0:128])
        P.vcopy(rowbuf[0:1, i * 128:(i + 1) * 128], pst[0:1, 0:128])
    return row_bcast(P, ctx, rowbuf[0:1, 0:tcw], tcw, bci=0)


def attn_core(P, ctx, mode, qsb, k_hbm, v_hbm, oacc, accs, qidx_sb, kidx_sb,
              colfac_sb=None, facK_sb=None, expscale=1.0):
    ksb = P.sb("at_k", [128, KT8 * TC], BF16)
    vsb = P.sb("at_v", [128, 4 * 1024], BF16)
    ssb = P.sb("at_s", [128, TC], BF16)
    asb = P.sb("at_a", [128, TC])
    msk = P.sb("at_m", [128, 4 * TC], BF16)
    qbc = P.sb("at_qbc", [128, TC])
    pso = P.ps("at_pso", [128, TC])
    P.memset(oacc[:], 0.0)
    for qc in range(NCH):
        P.memset(accs[:], 0.0)
        bq = row_bcast(P, ctx, qidx_sb[0:1, qc * TC:(qc + 1) * TC], TC, bci=0)
        P.vcopy(qbc[:], bq)
        for kc in range(NKC):
            for g in range(2):
                for f4 in range(4):
                    ft = g * 4 + f4
                    P.dma(ctx.stage[:, f4 * TC:(f4 + 1) * TC],
                          k_hbm[ft * 128:(ft + 1) * 128, kc * TC:(kc + 1) * TC])
                P.vcopy(ksb[:, g * 4 * TC:(g + 1) * 4 * TC], ctx.stage[:, 0:4 * TC])
            if facK_sb is not None:
                P.tt(ksb[:], ksb[:], facK_sb[:], ALU.mult)
            for tt_ in range(4):
                P.dma(ctx.stage[:, 0:1024],
                      v_hbm[kc * TC + tt_ * 128: kc * TC + (tt_ + 1) * 128, :])
                P.vcopy(vsb[:, tt_ * 1024:(tt_ + 1) * 1024], ctx.stage[:, 0:1024])
            for kt in range(4):
                P.ts(msk[:, kt * TC:(kt + 1) * TC], qbc[:],
                     kidx_sb[:, kc * 4 + kt: kc * 4 + kt + 1], None, ALU.is_ge)
            for h in range(H):
                po = 64 * (h % 2)
                hf = h // 2
                for kt in range(4):
                    pss = ctx.psb[0]
                    P.mm(pss[:, 0:TC],
                         ksb[po:po + 64, hf * TC + kt * 128: hf * TC + kt * 128 + 128],
                         qsb[po:po + 64, hf * NT + qc * TC: hf * NT + (qc + 1) * TC])
                    if mode == "soft":
                        P.act(ssb[:], pss[:, 0:TC], AF.Exp, scale=expscale)
                    else:
                        P.ts(ssb[:], pss[:, 0:TC],
                             colfac_sb[:, h * 64 + qc * 16 + kc: h * 64 + qc * 16 + kc + 1],
                             None, ALU.mult)
                    P.tt(ssb[:], ssb[:], msk[:, kt * TC:(kt + 1) * TC], ALU.mult)
                    P.act(asb[:], ssb[:], AF.Abs)
                    P.tt(accs[:, h * TC:(h + 1) * TC], accs[:, h * TC:(h + 1) * TC],
                         asb[:], ALU.add)
                    P.mm(pso[po:po + 64, 0:TC],
                         vsb[:, kt * 1024 + h * 64: kt * 1024 + h * 64 + 64],
                         ssb[:], start=(kt == 0), stop=(kt == 3))
                od = oacc[po:po + 64, hf * NT + qc * TC: hf * NT + (qc + 1) * TC]
                P.tt(od, od, pso[po:po + 64, 0:TC], ALU.add)
        for h in range(H):
            po = 64 * (h % 2)
            hf = h // 2
            pr = ctx.psr[0]
            P.mm(pr[:, 0:TC], ctx.ones[:, 0:1], accs[:, h * TC:(h + 1) * TC])
            r = ctx.rows[0]
            if mode == "retnet":
                P.ts(r[:, 0:TC], pr[:, 0:TC], 1.0, 50000.0, ALU.max, ALU.min)
            else:
                P.vcopy(r[:, 0:TC], pr[:, 0:TC])
            P.recip(r[:, 0:TC], r[:, 0:TC])
            bc = row_bcast(P, ctx, r[:, 0:TC], TC, nrows=64, bci=1, pbase=po)
            od = oacc[po:po + 64, hf * NT + qc * TC: hf * NT + (qc + 1) * TC]
            P.tt(od, od, bc, ALU.mult)


def b_l0a_proj():
    P = Prog()
    ctx = Ctx(P)
    x_h = P.din("x", [E, NT])
    w_h = {n: P.din(n, [E, E]) for n in ["Wq", "Wk", "Wv", "Wg", "WqP", "WkP"]}
    cos_h = P.din("cos2", [128, NT])
    sin_h = P.din("sin2", [128, NT])
    sgn_h = P.din("sgncol", [128, 1])
    qr_o = P.dout("qr", [E, NT])
    kr_o = P.dout("kr", [E, NT])
    v_o = P.dout("v", [NT, E])
    g_o = P.dout("g", [E, NT])

    xsb = P.sb("xsb", [128, KT8 * NT], BF16)
    aload(P, ctx, xsb, 0, x_h, 0, NT, NT)
    cossb = P.sb("cossb", [128, NT])
    sinsb = P.sb("sinsb", [128, NT])
    sgnsb = P.sb("sgnsb", [128, 1])
    P.dma(cossb[:], cos_h[:])
    P.dma(sinsb[:], sin_h[:])
    P.dma(sgnsb[:], sgn_h[:])
    qt = P.sb("qt", [128, NT])
    rt = P.sb("rt", [128, NT])

    scal = float(E) ** -0.5

    def proj_pair(wn, wpn, out_hbm, kscale):
        w = wload(P, ctx, "w_" + wn, w_h[wn], E, E)
        wp = wload(P, ctx, "w_" + wpn, w_h[wpn], E, E)
        for ft in range(KT8):
            for ci in range(NCH):
                def ev_q(nt, ps, ci=ci):
                    P.ts(qt[:, ci * TC:(ci + 1) * TC], ps, kscale, None, ALU.mult)

                def ev_r(nt, ps, ci=ci):
                    if kscale == 1.0:
                        P.ts(rt[:, ci * TC:(ci + 1) * TC], ps, sgnsb[:, 0:1], None, ALU.mult)
                    else:
                        P.ts(rt[:, ci * TC:(ci + 1) * TC], ps, sgnsb[:, 0:1], kscale,
                             ALU.mult, ALU.mult)

                for kt in range(KT8):
                    P.mm(ctx.psb[0][:, 0:TC], w[:, kt * E + ft * 128: kt * E + ft * 128 + 128],
                         xsb[:, kt * NT + ci * TC: kt * NT + (ci + 1) * TC],
                         start=(kt == 0), stop=(kt == KT8 - 1))
                ev_q(ft, ctx.psb[0][:, 0:TC])
                for kt in range(KT8):
                    P.mm(ctx.psb[1][:, 0:TC], wp[:, kt * E + ft * 128: kt * E + ft * 128 + 128],
                         xsb[:, kt * NT + ci * TC: kt * NT + (ci + 1) * TC],
                         start=(kt == 0), stop=(kt == KT8 - 1))
                ev_r(ft, ctx.psb[1][:, 0:TC])
            P.tt(qt[:], qt[:], cossb[:], ALU.mult)
            P.tt(rt[:], rt[:], sinsb[:], ALU.mult)
            P.tt(qt[:], qt[:], rt[:], ALU.add)
            P.dma(out_hbm[ft * 128:(ft + 1) * 128, :], qt[:])

    proj_pair("Wq", "WqP", qr_o, 1.0)
    proj_pair("Wk", "WkP", kr_o, scal)

    wg = wload(P, ctx, "w_g", w_h["Wg"], E, E)
    for ci in range(NCH):
        def ev_g(nt, ps, ci=ci):
            P.vcopy(qt[:, 0:TC], ps)
            P.dma(g_o[nt * 128:(nt + 1) * 128, ci * TC:(ci + 1) * TC], qt[:, 0:TC])

        mm_chain(P, ctx, xsb, NT, wg, E, E, ci * TC, TC, ev_g)

    wv = wload(P, ctx, "w_v", w_h["Wv"], E, E)
    for tti in range(NT // 128):
        for nc0 in range(0, E, TC):
            for kt in range(KT8):
                P.mm(ctx.psb[2][:, 0:TC],
                     xsb[:, kt * NT + tti * 128: kt * NT + (tti + 1) * 128],
                     wv[:, kt * E + nc0: kt * E + nc0 + TC],
                     start=(kt == 0), stop=(kt == KT8 - 1))
            P.vcopy(qt[:, 0:TC], ctx.psb[2][:, 0:TC])
            P.dma(v_o[tti * 128:(tti + 1) * 128, nc0:nc0 + TC], qt[:, 0:TC])
    return P.build()


def b_l0a_attn():
    P = Prog()
    ctx = Ctx(P, stage_w=2048)
    q_h = P.din("qr", [E, NT])
    k_h = P.din("krF", [E, S])
    v_h = P.din("vF", [S, E])
    g_h = P.din("g", [E, NT])
    x_h = P.din("x", [E, NT])
    facQ_h = P.din("facQ", [E, NT])
    facK_h = P.din("facK", [E, TC])
    colfac_h = P.din("colfac", [128, H * 64])
    qidx_h = P.din("qidx", [1, NT])
    kidx_h = P.din("kidx", [128, NKC * 4])
    out_h = P.dout("xo", [E, NT])

    qsb = P.sb("qsb", [128, KT8 * NT], BF16)
    for ft in range(KT8):
        for hv in range(2):
            c0 = hv * 1024
            P.dma(ctx.stage[:, 0:1024], q_h[ft * 128:(ft + 1) * 128, c0:c0 + 1024])
            P.dma(ctx.stage[:, 1024:2048], facQ_h[ft * 128:(ft + 1) * 128, c0:c0 + 1024])
            P.tt(ctx.stage[:, 0:1024], ctx.stage[:, 0:1024], ctx.stage[:, 1024:2048],
                 ALU.mult)
            P.vcopy(qsb[:, ft * NT + c0: ft * NT + c0 + 1024], ctx.stage[:, 0:1024])
    facKsb = P.sb("facKsb", [128, KT8 * TC], BF16)
    for ft in range(KT8):
        P.dma(ctx.stage[:, 0:TC], facK_h[ft * 128:(ft + 1) * 128, :])
        P.vcopy(facKsb[:, ft * TC:(ft + 1) * TC], ctx.stage[:, 0:TC])
    colsb = P.sb("colsb", [128, H * 64])
    P.dma(colsb[:], colfac_h[:])
    qidx = P.sb("qidxsb", [1, NT])
    P.dma(qidx[:], qidx_h[:])
    kidx = P.sb("kidxsb", [128, NKC * 4])
    P.dma(kidx[:], kidx_h[:])

    oacc = P.sb("oacc", [128, KT8 * NT])
    accs = P.sb("accs", [128, H * TC])
    attn_core(P, ctx, "retnet", qsb, k_h, v_h, oacc, accs, qidx, kidx,
              colfac_sb=colsb, facK_sb=facKsb)

    for qc in range(NCH):
        for h in range(H):
            po = 64 * (h % 2)
            hf = h // 2
            od = oacc[po:po + 64, hf * NT + qc * TC: hf * NT + (qc + 1) * TC]
            P.act(ctx.tmps[0][po:po + 64, 0:TC], od, AF.Square)
            pr = ctx.psr[0]
            P.mm(pr[:, 0:TC], ctx.ones[po:po + 64, 0:1], ctx.tmps[0][po:po + 64, 0:TC])
            r = ctx.rows[0]
            P.ts(r[:, 0:TC], pr[:, 0:TC], 1.0 / 64.0, 1e-6, ALU.mult, ALU.add)
            P.act(r[:, 0:TC], r[:, 0:TC], AF.Sqrt)
            P.recip(r[:, 0:TC], r[:, 0:TC])
            bc = row_bcast(P, ctx, r[:, 0:TC], TC, nrows=64, bci=1, pbase=po)
            P.tt(od, od, bc, ALU.mult)
    for ft in range(KT8):
        P.dma(ctx.stage[:, 0:NT], g_h[ft * 128:(ft + 1) * 128, :])
        P.act(ctx.stage[:, 0:NT], ctx.stage[:, 0:NT], AF.Silu)
        P.tt(oacc[:, ft * NT:(ft + 1) * NT], oacc[:, ft * NT:(ft + 1) * NT],
             ctx.stage[:, 0:NT], ALU.mult)
    residual_ln_store(P, ctx, oacc, NT, x_h, out_h)
    return P.build()


def b_soft_attn(biases, w_out_name, final_ln_gb):
    P = Prog()
    ctx = Ctx(P, stage_w=2048)
    q_h = P.din("q", [E, NT])
    k_h = P.din("kF", [E, S])
    v_h = P.din("vF", [S, E])
    x_h = P.din("x", [E, NT])
    wo_h = P.din("Wo", [E, E])
    bo_h = P.din("bo", [128, KT8])
    qidx_h = P.din("qidx", [1, NT])
    kidx_h = P.din("kidx", [128, NKC * 4])
    if final_ln_gb:
        lng_h = P.din("lng", [128, KT8])
        lnb_h = P.din("lnb", [128, KT8])
    out_h = P.dout("xo", [E, NT])

    qsb = P.sb("qsb", [128, KT8 * NT], BF16)
    aload(P, ctx, qsb, 0, q_h, 0, NT, NT)
    qidx = P.sb("qidxsb", [1, NT])
    P.dma(qidx[:], qidx_h[:])
    kidx = P.sb("kidxsb", [128, NKC * 4])
    P.dma(kidx[:], kidx_h[:])
    oacc = P.sb("oacc", [128, KT8 * NT])
    accs = P.sb("accs", [128, H * TC])
    attn_core(P, ctx, "soft", qsb, k_h, v_h, oacc, accs, qidx, kidx, expscale=0.125)

    for ft in range(KT8):
        P.vcopy(qsb[:, ft * NT:(ft + 1) * NT], oacc[:, ft * NT:(ft + 1) * NT])
    wo = wload(P, ctx, "w_o", wo_h, E, E)
    bo = P.sb("bosb", [128, KT8])
    P.dma(bo[:], bo_h[:])
    if final_ln_gb:
        lng = P.sb("lngsb", [128, KT8])
        lnb = P.sb("lnbsb", [128, KT8])
        P.dma(lng[:], lng_h[:])
        P.dma(lnb[:], lnb_h[:])
    for ci in range(NCH):
        def ev(nt, ps, ci=ci):
            d = oacc[:, nt * NT + ci * TC: nt * NT + ci * TC + TC]
            P.ts(d, ps, bo[:, nt:nt + 1], None, ALU.add)

        mm_chain(P, ctx, qsb, NT, wo, E, E, ci * TC, TC, ev)
    if final_ln_gb:
        for ci in range(NCH):
            tc0 = ci * TC
            for ft in range(KT8):
                P.dma(ctx.stage[:, 0:TC], x_h[ft * 128:(ft + 1) * 128, tc0:tc0 + TC])
                t = oacc[:, ft * NT + tc0: ft * NT + tc0 + TC]
                P.tt(t, t, ctx.stage[:, 0:TC], ALU.add)

            def store(ft, ap, tc0=tc0):
                P.dma(out_h[ft * 128:(ft + 1) * 128, tc0:tc0 + TC], ap)

            ln_chunk(P, ctx, oacc, NT, tc0, TC, 1e-6, store, gcol=lng, bcol=lnb)
    else:
        residual_ln_store(P, ctx, oacc, NT, x_h, out_h)
    return P.build()


def b_qkv_proj(with_bias):
    P = Prog()
    ctx = Ctx(P)
    x_h = P.din("x", [E, NT])
    wq_h = P.din("Wq", [E, E])
    wk_h = P.din("Wk", [E, E])
    wv_h = P.din("Wv", [E, E])
    if with_bias:
        bq_h = P.din("bq", [128, KT8])
        bk_h = P.din("bk", [128, KT8])
        bv_h = P.din("bv", [128, KT8])
    q_o = P.dout("q", [E, NT])
    k_o = P.dout("k", [E, NT])
    v_o = P.dout("v", [NT, E])

    xsb = P.sb("xsb", [128, KT8 * NT], BF16)
    aload(P, ctx, xsb, 0, x_h, 0, NT, NT)
    ob = P.sb("ob", [128, TC])
    if with_bias:
        bqs = P.sb("bqs", [128, KT8]); P.dma(bqs[:], bq_h[:])
        bks = P.sb("bks", [128, KT8]); P.dma(bks[:], bk_h[:])
        bvs = P.sb("bvs", [128, KT8]); P.dma(bvs[:], bv_h[:])

    def projfm(w_h, out_hbm, bsb, name):
        w = wload(P, ctx, name, w_h, E, E)
        for ci in range(NCH):
            def ev(nt, ps, ci=ci):
                if bsb is not None:
                    P.ts(ob[:, 0:TC], ps, bsb[:, nt:nt + 1], None, ALU.add)
                else:
                    P.vcopy(ob[:, 0:TC], ps)
                P.dma(out_hbm[nt * 128:(nt + 1) * 128, ci * TC:(ci + 1) * TC], ob[:, 0:TC])

            mm_chain(P, ctx, xsb, NT, w, E, E, ci * TC, TC, ev)

    projfm(wq_h, q_o, bqs if with_bias else None, "w_q")
    projfm(wk_h, k_o, bks if with_bias else None, "w_k")
    wv = wload(P, ctx, "w_v", wv_h, E, E)
    if with_bias:
        bvrow_h = P.din("bvrow", [1, E])
        brow = P.sb("brow", [1, E])
        P.dma(brow[:], bvrow_h[:])
    for tti in range(NT // 128):
        for j, nc0 in enumerate(range(0, E, TC)):
            for kt in range(KT8):
                P.mm(ctx.psb[2][:, 0:TC],
                     xsb[:, kt * NT + tti * 128: kt * NT + (tti + 1) * 128],
                     wv[:, kt * E + nc0: kt * E + nc0 + TC],
                     start=(kt == 0), stop=(kt == KT8 - 1))
            if with_bias:
                bc = row_bcast(P, ctx, brow[0:1, nc0:nc0 + TC], TC, bci=0)
                P.tt(ob[:, 0:TC], ctx.psb[2][:, 0:TC], bc, ALU.add)
            else:
                P.vcopy(ob[:, 0:TC], ctx.psb[2][:, 0:TC])
            P.dma(v_o[tti * 128:(tti + 1) * 128, nc0:nc0 + TC], ob[:, 0:TC])
    return P.build()


def topk_iter2(P, ctx, work, selmask, k):
    c0 = ctx.cols[0]
    eq = ctx.tmps[0][:, 0:work.shape[-1]]
    for _ in range(k):
        P.red(c0[:, 0:1], work, ALU.max)
        P.ts(eq, work, c0[:, 0:1], None, ALU.is_equal)
        P.tt(selmask, selmask, eq, ALU.max)
        P.stt(work, eq, -1e30, work, ALU.mult, ALU.add)


def b_l0f():
    P = Prog()
    ctx = Ctx(P, stage_w=2048)
    x_h = P.din("x", [E, NT])
    gw_h = P.din("gwT", [E, 32])
    eg_h = P.din("egT", [32, E, 512])
    eu_h = P.din("euT", [32, E, 512])
    ed_h = P.din("edT", [32, 512, E])
    sg_h = P.din("sgT", [E, 512])
    su_h = P.din("suT", [E, 512])
    sd_h = P.din("sdT", [512, E])
    G_h = P.din("G", [32, 8])
    G2_h = P.din("G2", [8, 32])
    ident_h = P.din("ident", [128, 128])
    out_h = P.dout("xo", [E, NT])

    xsb = P.sb("xsb", [128, KT8 * NT], BF16)
    aload(P, ctx, xsb, 0, x_h, 0, NT, NT)
    ident = P.sb("identsb", [128, 128])
    P.dma(ident[:], ident_h[:])
    Gsb = P.sb("Gsb", [32, 8])
    P.dma(Gsb[:], G_h[:])
    G2sb = P.sb("G2sb", [8, 32])
    P.dma(G2sb[:], G2_h[:])

    wg = wload(P, ctx, "w_gate", gw_h, E, 32)
    scfm = P.sb("scfm", [32, NT])
    for ci in range(NCH):
        ps = ctx.psb[0]
        for kt in range(KT8):
            P.mm(ps[0:32, 0:TC], wg[:, kt * 32:(kt + 1) * 32],
                 xsb[:, kt * NT + ci * TC:(kt * NT + (ci + 1) * TC)],
                 start=(kt == 0), stop=(kt == KT8 - 1))
        P.op("scalar", lambda e, ci=ci, ps=ps: e.activation(
            scfm[0:32, ci * TC:(ci + 1) * TC], ps[0:32, 0:TC], AF.Sigmoid))
    comb_tm = P.sb("compb", [128, (NT // 128) * 32])
    gsT = P.sb("gsT", [8, TC])
    gmT = P.sb("gmT", [8, TC])
    gs_tm = P.sb("gs_tm", [128, 8])
    gm_tm = P.sb("gm_tm", [128, 8])
    emT = P.sb("emT", [32, TC])
    ms_tm = P.sb("ms_tm", [128, 32])
    sel_tm = P.sb("sel_tm", [128, 32])
    pst = ctx.psb[3]
    for ci in range(NCH):
        P.mm(ctx.psb[1][0:8, 0:TC], Gsb[:], scfm[0:32, ci * TC:(ci + 1) * TC])
        P.vcopy(gsT[:, 0:TC], ctx.psb[1][0:8, 0:TC])
        for t4 in range(4):
            t = ci * 4 + t4
            P.trans(pst[0:128, 0:8], gsT[0:8, t4 * 128:(t4 + 1) * 128], ident[0:8, 0:8])
            P.vcopy(gs_tm[:], pst[0:128, 0:8])
            P.memset(gm_tm[:], 0.0)
            topk_iter2(P, ctx, gs_tm[:], gm_tm[:], 4)
            P.trans(pst[0:8, 0:128], gm_tm[:], ident[0:128, 0:128])
            P.vcopy(gmT[0:8, t4 * 128:(t4 + 1) * 128], pst[0:8, 0:128])
        P.mm(ctx.psb[1][0:32, 0:TC], G2sb[:], gmT[0:8, 0:TC])
        P.tt(emT[0:32, 0:TC], ctx.psb[1][0:32, 0:TC],
             scfm[0:32, ci * TC:(ci + 1) * TC], ALU.mult)
        for t4 in range(4):
            t = ci * 4 + t4
            P.trans(pst[0:128, 0:32], emT[0:32, t4 * 128:(t4 + 1) * 128], ident[0:32, 0:32])
            P.vcopy(ms_tm[:], pst[0:128, 0:32])
            P.memset(sel_tm[:], 0.0)
            c_ms = ctx.tmps[1][:, 0:32]
            P.vcopy(c_ms, ms_tm[:])
            topk_iter2(P, ctx, c_ms, sel_tm[:], 8)
            P.tt(ms_tm[:], ms_tm[:], sel_tm[:], ALU.mult)
            P.red(ctx.cols[1][:, 0:1], ms_tm[:], ALU.add)
            P.ts(ctx.cols[1][:, 0:1], ctx.cols[1][:, 0:1], 1e-12, None, ALU.max)
            P.recip(ctx.cols[1][:, 0:1], ctx.cols[1][:, 0:1])
            P.ts(comb_tm[:, t * 32:(t + 1) * 32], ms_tm[:], ctx.cols[1][:, 0:1],
                 None, ALU.mult)

    acc = P.sb("acc", [128, KT8 * NT])
    P.memset(acc[:], 0.0)
    wga = P.sb("wga", [128, KT8 * 512], BF16)
    wua = P.sb("wua", [128, KT8 * 512], BF16)
    wda = P.sb("wda", [128, 4 * E], BF16)
    hsb = P.sb("hsb", [128, 4 * TC], BF16)
    growb = P.sb("growb", [1, TC])

    def expert(eidx):
        shared = eidx < 0
        if shared:
            wfill(P, ctx, wga, sg_h, E, 512)
            wfill(P, ctx, wua, su_h, E, 512)
            wfill(P, ctx, wda, sd_h, 512, E)
        else:
            wfill(P, ctx, wga, eg_h[eidx], E, 512)
            wfill(P, ctx, wua, eu_h[eidx], E, 512)
            wfill(P, ctx, wda, ed_h[eidx], 512, E)
        for ci in range(NCH):
            for it in range(4):
                for kt in range(KT8):
                    P.mm(ctx.psb[0][:, 0:TC],
                         wga[:, kt * 512 + it * 128: kt * 512 + it * 128 + 128],
                         xsb[:, kt * NT + ci * TC: kt * NT + (ci + 1) * TC],
                         start=(kt == 0), stop=(kt == KT8 - 1))
                    P.mm(ctx.psb[1][:, 0:TC],
                         wua[:, kt * 512 + it * 128: kt * 512 + it * 128 + 128],
                         xsb[:, kt * NT + ci * TC: kt * NT + (ci + 1) * TC],
                         start=(kt == 0), stop=(kt == KT8 - 1))
                P.act(ctx.tmps[0][:, 0:TC], ctx.psb[0][:, 0:TC], AF.Silu)
                P.tt(hsb[:, it * TC:(it + 1) * TC], ctx.tmps[0][:, 0:TC],
                     ctx.psb[1][:, 0:TC], ALU.mult)
            if not shared:
                bc = gate_row_bc(P, ctx, comb_tm, 32, eidx, ci * TC, TC, ident,
                                 ctx.psb[3], growb)
            for dt in range(KT8):
                for kt in range(4):
                    P.mm(ctx.psb[2][:, 0:TC],
                         wda[:, kt * E + dt * 128: kt * E + dt * 128 + 128],
                         hsb[:, kt * TC:(kt + 1) * TC],
                         start=(kt == 0), stop=(kt == 3))
                a = acc[:, dt * NT + ci * TC: dt * NT + (ci + 1) * TC]
                if shared:
                    P.tt(a, a, ctx.psb[2][:, 0:TC], ALU.add)
                else:
                    P.tt(ctx.tmps[0][:, 0:TC], ctx.psb[2][:, 0:TC], bc, ALU.mult)
                    P.tt(a, a, ctx.tmps[0][:, 0:TC], ALU.add)

    for e in range(32):
        expert(e)
    expert(-1)
    residual_ln_store(P, ctx, acc, NT, x_h, out_h)
    return P.build()


def b_l1f_moe():
    P = Prog()
    ctx = Ctx(P, stage_w=2048)
    o1_h = P.din("out1", [E, NT])
    x3_h = P.din("x3", [E, NT])
    wr_h = P.din("Wr", [E, 8])
    we1_h = P.din("We1", [8, E, 4096])
    be1_h = P.din("be1r", [8, 128, 32])
    we2_h = P.din("We2", [8, 4096, E])
    be2_h = P.din("be2r", [8, 128, 8])
    g2_h = P.din("ln2g", [128, KT8])
    b2_h = P.din("ln2b", [128, KT8])
    ident_h = P.din("ident", [128, 128])
    macc_h = P.dout("macc", [E, NT])
    out_h = P.dout("xo", [E, NT])

    ident = P.sb("identsb", [128, 128])
    P.dma(ident[:], ident_h[:])
    xc = P.sb("xc", [128, KT8 * TC], BF16)
    gates_tm = P.sb("gates_tm", [128, (NT // 128) * 8])
    wr = wload(P, ctx, "w_r", wr_h, E, 8)
    lgT = P.sb("lgT", [8, TC])
    lg_tm = P.sb("lg_tm", [128, 8])
    sel_tm = P.sb("sel_tm", [128, 8])
    pst = ctx.psb[3]
    for ci in range(NCH):
        aload(P, ctx, xc, 0, o1_h, ci * TC, TC, TC)
        ps = ctx.psb[0]
        for kt in range(KT8):
            P.mm(ps[0:8, 0:TC], wr[:, kt * 8:(kt + 1) * 8],
                 xc[:, kt * TC:(kt + 1) * TC], start=(kt == 0), stop=(kt == KT8 - 1))
        P.vcopy(lgT[:, 0:TC], ps[0:8, 0:TC])
        for t4 in range(4):
            t = ci * 4 + t4
            P.trans(pst[0:128, 0:8], lgT[0:8, t4 * 128:(t4 + 1) * 128], ident[0:8, 0:8])
            P.vcopy(lg_tm[:], pst[0:128, 0:8])
            P.red(ctx.cols[0][:, 0:1], lg_tm[:], ALU.max)
            P.ts(ctx.cols[0][:, 0:1], ctx.cols[0][:, 0:1], -1.0, None, ALU.mult)
            P.op("scalar", lambda e: e.activation(lg_tm[:], lg_tm[:], AF.Exp,
                                                  bias=ctx.cols[0][:, 0:1], scale=1.0))
            P.red(ctx.cols[1][:, 0:1], lg_tm[:], ALU.add)
            P.recip(ctx.cols[1][:, 0:1], ctx.cols[1][:, 0:1])
            P.ts(lg_tm[:], lg_tm[:], ctx.cols[1][:, 0:1], None, ALU.mult)
            P.memset(sel_tm[:], 0.0)
            c_p = ctx.tmps[1][:, 0:8]
            P.vcopy(c_p, lg_tm[:])
            topk_iter2(P, ctx, c_p, sel_tm[:], 2)
            P.op("scalar", lambda e: e.activation(ctx.tmps[0][:, 0:8], lg_tm[:], AF.Exp))
            P.tt(ctx.tmps[0][:, 0:8], ctx.tmps[0][:, 0:8], sel_tm[:], ALU.mult)
            P.red(ctx.cols[1][:, 0:1], ctx.tmps[0][:, 0:8], ALU.add)
            P.recip(ctx.cols[1][:, 0:1], ctx.cols[1][:, 0:1])
            P.ts(gates_tm[:, t * 8:(t + 1) * 8], ctx.tmps[0][:, 0:8],
                 ctx.cols[1][:, 0:1], None, ALU.mult)

    we1 = P.sb("we1", [128, KT8 * 4096], BF16)
    we2 = P.sb("we2", [128, 32 * E], BF16)
    hsb = P.sb("hsb", [128, 32 * TC], BF16)
    b1c = P.sb("b1c", [128, 32])
    b2c = P.sb("b2c", [128, 8])
    growb = P.sb("growb", [1, TC])
    for e in range(8):
        wfill(P, ctx, we1, we1_h[e], E, 4096)
        wfill(P, ctx, we2, we2_h[e], 4096, E)
        P.dma(b1c[:], be1_h[e])
        P.dma(b2c[:], be2_h[e])
        for ci in range(NCH):
            aload(P, ctx, xc, 0, o1_h, ci * TC, TC, TC)
            for nt in range(32):
                for kt in range(KT8):
                    P.mm(ctx.psb[0][:, 0:TC],
                         we1[:, kt * 4096 + nt * 128: kt * 4096 + nt * 128 + 128],
                         xc[:, kt * TC:(kt + 1) * TC],
                         start=(kt == 0), stop=(kt == KT8 - 1))
                P.op("scalar", lambda e2, nt=nt: e2.activation(
                    hsb[:, nt * TC:(nt + 1) * TC], ctx.psb[0][:, 0:TC], AF.Gelu,
                    bias=b1c[:, nt:nt + 1], scale=1.0))
            bc = gate_row_bc(P, ctx, gates_tm, 8, e, ci * TC, TC, ident,
                             ctx.psb[3], growb)
            for dt in range(KT8):
                for kt in range(32):
                    P.mm(ctx.psb[2][:, 0:TC],
                         we2[:, kt * E + dt * 128: kt * E + dt * 128 + 128],
                         hsb[:, kt * TC:(kt + 1) * TC],
                         start=(kt == 0), stop=(kt == 31))
                P.ts(ctx.tmps[0][:, 0:TC], ctx.psb[2][:, 0:TC], b2c[:, dt:dt + 1],
                     None, ALU.add)
                P.tt(ctx.tmps[0][:, 0:TC], ctx.tmps[0][:, 0:TC], bc, ALU.mult)
                dst = macc_h[dt * 128:(dt + 1) * 128, ci * TC:(ci + 1) * TC]
                if e == 0:
                    P.dma(dst, ctx.tmps[0][:, 0:TC])
                else:
                    P.dma(dst, ctx.tmps[0][:, 0:TC], q="gpsimd", accum_op=ALU.add)

    g2c = P.sb("g2c", [128, KT8])
    b2cc = P.sb("b2cc", [128, KT8])
    P.dma(g2c[:], g2_h[:])
    P.dma(b2cc[:], b2_h[:])
    TH = 256
    fb = P.sb("fb", [128, KT8 * TH])
    for ci in range(NT // TH):
        for ft in range(KT8):
            P.dma(ctx.stage[:, 0:TH], macc_h[ft * 128:(ft + 1) * 128,
                                             ci * TH:(ci + 1) * TH])
            P.dma(ctx.tmps[0][:, 0:TH], o1_h[ft * 128:(ft + 1) * 128,
                                             ci * TH:(ci + 1) * TH])
            P.tt(fb[:, ft * TH:(ft + 1) * TH], ctx.stage[:, 0:TH],
                 ctx.tmps[0][:, 0:TH], ALU.add)

        def add_x3(ft, ap, ci=ci):
            P.dma(ctx.stage[:, 0:TH], x3_h[ft * 128:(ft + 1) * 128,
                                           ci * TH:(ci + 1) * TH])
            P.tt(ap, ap, ctx.stage[:, 0:TH], ALU.add)

        ln_chunk(P, ctx, fb, TH, 0, TH, 1e-6, add_x3, gcol=g2c, bcol=b2cc)

        def store(ft, ap, ci=ci):
            P.dma(out_h[ft * 128:(ft + 1) * 128, ci * TH:(ci + 1) * TH], ap)

        ln_chunk(P, ctx, fb, TH, 0, TH, 1e-5, store)
    return P.build()


def b_l2f_a():
    P = Prog()
    ctx = Ctx(P, stage_w=2048)
    x_h = P.din("x", [E, NT])
    g_h = P.din("lng", [128, KT8])
    b_h = P.din("lnb", [128, KT8])
    q1_h = P.din("fc1q", [E, 4096], I32)
    s1_h = P.din("fc1s", [8, 4096])
    b1_h = P.din("b1r", [128, 32])
    y_o = P.dout("y", [4096, NT], BF16)

    gc = P.sb("gc", [128, KT8]); P.dma(gc[:], g_h[:])
    bc_ = P.sb("bc_", [128, KT8]); P.dma(bc_[:], b_h[:])
    b1c = P.sb("b1c", [128, 32]); P.dma(b1c[:], b1_h[:])
    w1 = P.sb("w1", [128, KT8 * 4096], BF16)
    sti = P.sb("sti", [128, 2048], I32)
    srow = P.sb("srow", [1, 4096])
    for kt in range(KT8):
        P.dma(srow[:], s1_h[kt:kt + 1, :])
        for half in range(2):
            c0 = half * 2048
            P.dma(sti[:], q1_h[kt * 128:(kt + 1) * 128, c0:c0 + 2048])
            P.scopy(ctx.stage[:, 0:2048], sti[:])
            P.ts(ctx.stage[:, 0:2048], ctx.stage[:, 0:2048], -128.0, None, ALU.add)
            for p in range(4):
                bcx = row_bcast(P, ctx, srow[0:1, c0 + p * TC: c0 + (p + 1) * TC], TC, bci=0)
                P.tt(w1[:, kt * 4096 + c0 + p * TC: kt * 4096 + c0 + (p + 1) * TC],
                     ctx.stage[:, p * TC:(p + 1) * TC], bcx, ALU.mult)
    fb = P.sb("fb", [128, KT8 * TC])
    hc = P.sb("hc", [128, KT8 * TC], BF16)
    yc = P.sb("yc", [128, TC], BF16)
    for ci in range(NCH):
        for ft in range(KT8):
            P.dma(fb[:, ft * TC:(ft + 1) * TC],
                  x_h[ft * 128:(ft + 1) * 128, ci * TC:(ci + 1) * TC])

        def tobf(ft, ap):
            P.vcopy(hc[:, ft * TC:(ft + 1) * TC], ap)

        ln_chunk(P, ctx, fb, TC, 0, TC, 1e-5, tobf, gcol=gc, bcol=bc_)
        for nt in range(32):
            for kt in range(KT8):
                P.mm(ctx.psb[0][:, 0:TC],
                     w1[:, kt * 4096 + nt * 128: kt * 4096 + nt * 128 + 128],
                     hc[:, kt * TC:(kt + 1) * TC],
                     start=(kt == 0), stop=(kt == KT8 - 1))
            P.op("scalar", lambda e, nt=nt: e.activation(
                yc[:, 0:TC], ctx.psb[0][:, 0:TC], AF.Silu,
                bias=b1c[:, nt:nt + 1], scale=1.0))
            P.dma(y_o[nt * 128:(nt + 1) * 128, ci * TC:(ci + 1) * TC], yc[:, 0:TC])
    return P.build()


def b_l2f_b():
    P = Prog()
    ctx = Ctx(P, stage_w=2048)
    x_h = P.din("x", [E, NT])
    y_h = P.din("y", [4096, NT], BF16)
    q2_h = P.din("fc2q", [4096, E], I32)
    s2_h = P.din("fc2s", [32, E])
    b2_h = P.din("b2r", [128, 8])
    out_h = P.dout("xo", [E, NT])

    b2c = P.sb("b2c", [128, 8]); P.dma(b2c[:], b2_h[:])
    w2 = P.sb("w2", [128, 32 * E], BF16)
    sti = P.sb("sti", [128, 2048], I32)
    srow = P.sb("srow", [1, E])
    for kt in range(32):
        P.dma(srow[:], s2_h[kt:kt + 1, :])
        P.dma(sti[:, 0:E], q2_h[kt * 128:(kt + 1) * 128, :])
        P.scopy(ctx.stage[:, 0:E], sti[:, 0:E])
        P.ts(ctx.stage[:, 0:E], ctx.stage[:, 0:E], -128.0, None, ALU.add)
        for p in range(2):
            bcx = row_bcast(P, ctx, srow[0:1, p * TC:(p + 1) * TC], TC, bci=0)
            P.tt(w2[:, kt * E + p * TC: kt * E + (p + 1) * TC],
                 ctx.stage[:, p * TC:(p + 1) * TC], bcx, ALU.mult)
    yc = P.sb("ycb", [128, 32 * TC], BF16)
    fb = P.sb("fb", [128, KT8 * TC])
    for ci in range(NCH):
        for nt in range(32):
            P.dma(yc[:, nt * TC:(nt + 1) * TC],
                  y_h[nt * 128:(nt + 1) * 128, ci * TC:(ci + 1) * TC])
        for dt in range(KT8):
            for kt in range(32):
                P.mm(ctx.psb[0][:, 0:TC],
                     w2[:, kt * E + dt * 128: kt * E + dt * 128 + 128],
                     yc[:, kt * TC:(kt + 1) * TC],
                     start=(kt == 0), stop=(kt == 31))
            P.ts(ctx.tmps[0][:, 0:TC], ctx.psb[0][:, 0:TC], b2c[:, dt:dt + 1],
                 None, ALU.add)
            P.dma(ctx.stage[:, 0:TC], x_h[dt * 128:(dt + 1) * 128,
                                          ci * TC:(ci + 1) * TC])
            P.ts(ctx.tmps[0][:, 0:TC], ctx.tmps[0][:, 0:TC], 0.5, None, ALU.mult)
            P.stt(fb[:, dt * TC:(dt + 1) * TC], ctx.stage[:, 0:TC], 2.0,
                  ctx.tmps[0][:, 0:TC], ALU.mult, ALU.add)

        def store(ft, ap, ci=ci):
            P.dma(out_h[ft * 128:(ft + 1) * 128, ci * TC:(ci + 1) * TC], ap)

        ln_chunk(P, ctx, fb, TC, 0, TC, 1e-5, store)
    return P.build()


def b_l3a_proj():
    P = Prog()
    ctx = Ctx(P)
    x_h = P.din("x", [E, NT])
    w_h = P.din("wqkv", [E, 3 * E])
    q_o = P.dout("q", [E, NT])
    k_o = P.dout("k", [E, NT])
    v_o = P.dout("v", [NT, E])
    ql_o = P.dout("qlp", [E, 16])
    kl_o = P.dout("klp", [E, 16])

    xsb = P.sb("xsb", [128, KT8 * NT], BF16)
    aload(P, ctx, xsb, 0, x_h, 0, NT, NT)
    w = wload(P, ctx, "w_qkv", w_h, E, 3 * E)
    ob = P.sb("ob", [128, TC])
    qlsb = P.sb("qlsb", [128, KT8 * 16])
    klsb = P.sb("klsb", [128, KT8 * 16])

    def fmproj(col0, out_hbm, scale, lsb):
        for ci in range(NCH):
            for nt in range(KT8):
                for kt in range(KT8):
                    P.mm(ctx.psb[0][:, 0:TC],
                         w[:, kt * 3 * E + col0 + nt * 128: kt * 3 * E + col0 + nt * 128 + 128],
                         xsb[:, kt * NT + ci * TC: kt * NT + (ci + 1) * TC],
                         start=(kt == 0), stop=(kt == KT8 - 1))
                P.ts(ob[:, 0:TC], ctx.psb[0][:, 0:TC], scale, None, ALU.mult)
                P.dma(out_hbm[nt * 128:(nt + 1) * 128, ci * TC:(ci + 1) * TC], ob[:, 0:TC])
                if lsb is not None:
                    for j in range(4):
                        P.red(lsb[:, nt * 16 + ci * 4 + j: nt * 16 + ci * 4 + j + 1],
                              ob[:, j * 128:(j + 1) * 128], ALU.add)

    fmproj(0, q_o, 0.125, qlsb)
    fmproj(E, k_o, 1.0, klsb)
    P.ts(qlsb[:], qlsb[:], 1.0 / 128.0, None, ALU.mult)
    P.ts(klsb[:], klsb[:], 1.0 / 128.0, None, ALU.mult)
    for nt in range(KT8):
        P.dma(ql_o[nt * 128:(nt + 1) * 128, :], qlsb[:, nt * 16:(nt + 1) * 16])
        P.dma(kl_o[nt * 128:(nt + 1) * 128, :], klsb[:, nt * 16:(nt + 1) * 16])
    for tti in range(NT // 128):
        for j, nc0 in enumerate(range(0, E, TC)):
            for kt in range(KT8):
                P.mm(ctx.psb[2][:, 0:TC],
                     xsb[:, kt * NT + tti * 128: kt * NT + (tti + 1) * 128],
                     w[:, kt * 3 * E + 2 * E + nc0: kt * 3 * E + 2 * E + nc0 + TC],
                     start=(kt == 0), stop=(kt == KT8 - 1))
            P.vcopy(ob[:, 0:TC], ctx.psb[2][:, 0:TC])
            P.dma(v_o[tti * 128:(tti + 1) * 128, nc0:nc0 + TC], ob[:, 0:TC])
    return P.build()


def b_l3a_core():
    P = Prog()
    ctx = Ctx(P, stage_w=2048)
    q_h = P.din("q", [E, NT])
    k_h = P.din("kF", [E, S])
    v_h = P.din("vF", [S, E])
    ql_h = P.din("Ql", [E, 64])
    kl_h = P.din("Kl", [E, 64])
    x_h = P.din("x", [E, NT])
    wp_h = P.din("wproj", [E, E])
    ident_h = P.din("ident", [128, 128])
    e7_h = P.din("eye7", [64, 64])
    e15_h = P.din("eye15", [64, 64])
    e13_h = P.din("eye13", [64, 64])
    out_h = P.dout("xo", [E, NT])

    ident = P.sb("identsb", [128, 128]); P.dma(ident[:], ident_h[:])
    e7 = P.sb("e7", [64, 64]); P.dma(e7[:], e7_h[:])
    e15 = P.sb("e15", [64, 64]); P.dma(e15[:], e15_h[:])
    e13 = P.sb("e13", [64, 64]); P.dma(e13[:], e13_h[:])
    qlf = P.sb("qlf", [128, KT8 * 64]); 
    klf = P.sb("klf", [128, KT8 * 64])
    for ft in range(KT8):
        P.dma(qlf[:, ft * 64:(ft + 1) * 64], ql_h[ft * 128:(ft + 1) * 128, :])
        P.dma(klf[:, ft * 64:(ft + 1) * 64], kl_h[ft * 128:(ft + 1) * 128, :])
    Af = P.sb("Af", [64, H * 64])

    qlh0 = P.sb("qlh0", [64, 64])
    klh0 = P.sb("klh0", [64, 64])
    k2sb = P.sb("k2sb", [64, 64])
    k2T = P.sb("k2T", [64, 64])
    KVsb = P.sb("KVsb", [64, 64])
    KVT = P.sb("KVT", [64, 64])
    T1 = P.sb("T1", [64, 64])
    T3 = P.sb("T3", [64, 64])
    T4 = P.sb("T4", [64, 64])
    Vsb = P.sb("Vsb", [64, 64])
    VT = P.sb("VT", [64, 64])
    Bn = P.sb("Bn", [64, 64])
    ksb3 = P.sb("ksb3", [64, TC])
    p3 = P.sb("p3", [64, TC])
    pT = P.sb("pT", [128, 64])
    vh = P.sb("vh", [128, 64])
    runmax = P.sb("runmax", [64, 1])
    sumcol = P.sb("sumcol", [64, 1])
    negm = P.sb("negm", [64, 1])
    cA = P.sb("cA", [64, 1])
    d0 = P.sb("d0", [1, 1])
    psmall = P.ps("psmall", [128, 128])
    psB = P.ps("psB", [64, 64])

    def sm_free(buf, n, colmax, colsum):
        P.red(colmax[:, 0:1], buf, ALU.max)
        P.ts(colmax[:, 0:1], colmax[:, 0:1], -1.0, None, ALU.mult)
        P.op("scalar", lambda e: e.activation(buf, buf, AF.Exp, bias=colmax[:, 0:1],
                                              scale=1.0))
        P.red(colsum[:, 0:1], buf, ALU.add)
        P.recip(colsum[:, 0:1], colsum[:, 0:1])
        P.op("vector", lambda e: e.tensor_scalar(buf, buf, colsum[:, 0:1], None,
                                                 ALU.mult, ALU.bypass))

    for h in range(H):
        po, hf = 64 * (h % 2), h // 2
        P.dma(qlh0[:], qlf[po:po + 64, hf * 64:(hf + 1) * 64])
        P.dma(klh0[:], klf[po:po + 64, hf * 64:(hf + 1) * 64])
        P.mm(psmall[0:64, 0:64], qlh0[:], klh0[:])
        P.vcopy(k2sb[:], psmall[0:64, 0:64])
        sm_free(k2sb[:], 64, runmax, sumcol)
        P.mm(ctx.psr[0][:, 0:64], ctx.ones[0:64, 0:1], k2sb[:])
        P.red(d0[:, 0:1], ctx.psr[0][0:1, 0:64], ALU.max)
        P.recip(d0[:, 0:1], d0[:, 0:1])
        P.trans(psmall[0:64, 0:64], k2sb[:], ident[0:64, 0:64])
        P.vcopy(k2T[:], psmall[0:64, 0:64])
        P.mm(psmall[0:64, 0:1], ctx.ones[0:1, 0:64], d0[0:1, 0:1])
        P.vcopy(cA[:, 0:1], psmall[0:64, 0:1])
        P.ts(Vsb[:], k2T[:], cA[:, 0:1], None, ALU.mult)
        for _ in range(6):
            P.mm(psmall[0:64, 0:64], k2T[:], Vsb[:])
            P.vcopy(KVsb[:], psmall[0:64, 0:64])
            P.trans(psmall[0:64, 0:64], KVsb[:], ident[0:64, 0:64])
            P.vcopy(KVT[:], psmall[0:64, 0:64])
            P.tt(T1[:], e7[:], KVsb[:], ALU.subtract)
            P.mm(psmall[0:64, 0:64], KVT[:], T1[:])
            P.tt(T3[:], e15[:], psmall[0:64, 0:64], ALU.subtract)
            P.mm(psmall[0:64, 0:64], KVT[:], T3[:])
            P.tt(T4[:], e13[:], psmall[0:64, 0:64], ALU.subtract)
            P.trans(psmall[0:64, 0:64], Vsb[:], ident[0:64, 0:64])
            P.vcopy(VT[:], psmall[0:64, 0:64])
            P.mm(psmall[0:64, 0:64], VT[:], T4[:])
            P.ts(Vsb[:], psmall[0:64, 0:64], 0.25, None, ALU.mult)
        P.memset(runmax[:, 0:1], -3e38)
        for kc in range(NKC):
            P.dma(ksb3[:], k_h[h * 64:(h + 1) * 64, kc * TC:(kc + 1) * TC])
            P.mm(ctx.psb[0][0:64, 0:TC], qlh0[:], ksb3[:])
            P.red(cA[:, 0:1], ctx.psb[0][0:64, 0:TC], ALU.max)
            P.tt(runmax[:, 0:1], runmax[:, 0:1], cA[:, 0:1], ALU.max)
        P.ts(negm[:, 0:1], runmax[:, 0:1], -1.0, None, ALU.mult)
        P.memset(sumcol[:, 0:1], 0.0)
        for kc in range(NKC):
            P.dma(ksb3[:], k_h[h * 64:(h + 1) * 64, kc * TC:(kc + 1) * TC])
            P.mm(ctx.psb[0][0:64, 0:TC], qlh0[:], ksb3[:])
            P.op("scalar", lambda e: e.activation(p3[:], ctx.psb[0][0:64, 0:TC],
                                                  AF.Exp, bias=negm[:, 0:1], scale=1.0))
            P.red(cA[:, 0:1], p3[:], ALU.add)
            P.tt(sumcol[:, 0:1], sumcol[:, 0:1], cA[:, 0:1], ALU.add)
            for t4 in range(4):
                P.trans(psmall[0:128, 0:64], p3[0:64, t4 * 128:(t4 + 1) * 128], ident[0:64, 0:64])
                P.vcopy(pT[:], psmall[0:128, 0:64])
                P.dma(vh[:], v_h[kc * TC + t4 * 128: kc * TC + (t4 + 1) * 128,
                                 h * 64:(h + 1) * 64])
                P.mm(psB[:, 0:64], pT[:], vh[:],
                     start=(kc == 0 and t4 == 0), stop=(kc == NKC - 1 and t4 == 3))
        P.recip(cA[:, 0:1], sumcol[:, 0:1])
        P.ts(Bn[:], psB[:, 0:64], cA[:, 0:1], None, ALU.mult)
        P.trans(psmall[0:64, 0:64], Vsb[:], ident[0:64, 0:64])
        P.vcopy(VT[:], psmall[0:64, 0:64])
        P.mm(psmall[0:64, 0:64], VT[:], Bn[:])
        P.vcopy(Af[:, h * 64:(h + 1) * 64], psmall[0:64, 0:64])

    qfsb = P.sb("qfsb", [128, KT8 * NT])
    for ft in range(KT8):
        P.dma(qfsb[:, ft * NT:(ft + 1) * NT], q_h[ft * 128:(ft + 1) * 128, :])
    svf = P.sb("svf", [128, KT8 * NT])
    qh0 = P.sb("qh0", [64, 128])
    k1sb = P.sb("k1sb", [128, 64])
    k1T = P.sb("k1T", [64, 128])
    cB = P.sb("cB", [128, 1])
    cC = P.sb("cC", [128, 1])
    for t in range(NT // 128):
        for h in range(H):
            po, hf = 64 * (h % 2), h // 2
            P.dma(qh0[:], qfsb[po:po + 64, hf * NT + t * 128: hf * NT + (t + 1) * 128])
            P.dma(klh0[:], klf[po:po + 64, hf * 64:(hf + 1) * 64])
            P.mm(psmall[0:128, 0:64], qh0[:], klh0[:])
            P.vcopy(k1sb[:], psmall[0:128, 0:64])
            sm_free(k1sb[:], 64, cB, cC)
            P.trans(psmall[0:64, 0:128], k1sb[:], ident[0:128, 0:128])
            P.vcopy(k1T[:], psmall[0:64, 0:128])
            P.mm(psmall[po:po + 64, 0:128], Af[:, h * 64:(h + 1) * 64], k1T[:])
            P.vcopy(svf[po:po + 64, hf * NT + t * 128: hf * NT + (t + 1) * 128],
                    psmall[po:po + 64, 0:128])
    wp = wload(P, ctx, "w_p", wp_h, E, E)
    svc = P.sb("svc", [128, KT8 * TC], BF16)
    for ci in range(NCH):
        for ft in range(KT8):
            P.vcopy(svc[:, ft * TC:(ft + 1) * TC],
                    svf[:, ft * NT + ci * TC: ft * NT + (ci + 1) * TC])

        def ev(nt, ps, ci=ci):
            P.vcopy(qfsb[:, nt * NT + ci * TC: nt * NT + (ci + 1) * TC], ps)

        mm_chain(P, ctx, svc, TC, wp, E, E, 0, TC, ev)
    residual_ln_store(P, ctx, qfsb, NT, x_h, out_h)
    return P.build()


def b_l3f():
    P = Prog()
    ctx = Ctx(P, stage_w=2048)
    x_h = P.din("x", [E, NT])
    r1_h = P.dout("r1", [E, NT])
    rw_h = P.din("rW", [2, E, 8])
    rb_h = P.din("rb", [2, 8, 1])
    gb_h = P.din("gbias", [2, 1, 8])
    wup_h = P.din("eWup", [6, E, E])
    bup_h = P.din("ebupr", [6, 128, 8])
    wg_h = P.din("eWg", [6, E, E])
    bg_h = P.din("ebgr", [6, 128, 8])
    wd_h = P.din("eWd", [6, E, E])
    bd_h = P.din("ebdr", [6, 128, 8])
    sup_h = P.din("sWup", [E, 2048])
    sbup_h = P.din("sbupr", [2, 128, 8])
    swg_h = P.din("sWg", [E, E])
    sbg_h = P.din("sbgr", [128, 8])
    swd_h = P.din("sWd", [E, E])
    sbd_h = P.din("sbdr", [128, 8])
    nw_h = P.din("normw", [2, 128, 8])
    ident_h = P.din("ident", [128, 128])
    out_h = P.dout("xo", [E, NT])

    ident = P.sb("identsb", [128, 128]); P.dma(ident[:], ident_h[:])
    xfb = P.sb("xfb", [128, KT8 * NT], BF16)
    acc = P.sb("acc", [128, KT8 * NT])
    wup = P.sb("wup", [128, KT8 * E], BF16)
    wgg = P.sb("wgg", [128, KT8 * E], BF16)
    wdd = P.sb("wdd", [128, KT8 * E], BF16)
    hb = P.sb("hb", [128, KT8 * TC], BF16)
    g2b = P.sb("g2b", [128, KT8 * TC], BF16)
    bupc = P.sb("bupc", [128, 8])
    bgc = P.sb("bgc", [128, 8])
    bdc = P.sb("bdc", [128, 8])
    nwc = P.sb("nwc", [128, 8])
    w_tm = P.sb("w_tm", [128, (NT // 128) * 8])
    lgT = P.sb("lgT", [8, TC])
    lg_tm = P.sb("lg_tm", [128, 8])
    bi_tm = P.sb("bi_tm", [128, 8])
    sel_tm = P.sb("sel_tm", [128, 8])
    gbbc = P.sb("gbbc", [128, 8])
    growb = P.sb("growb", [1, TC])
    sb16 = P.sb("sb16", [128, 16 * TC], BF16)

    def one_round(j, in_hbm, rstore):
        for ft in range(KT8):
            P.dma(ctx.stage[:, 0:NT], in_hbm[ft * 128:(ft + 1) * 128, :])
            P.vcopy(xfb[:, ft * NT:(ft + 1) * NT], ctx.stage[:, 0:NT])
        P.memset(acc[:], 0.0)
        wr = wload(P, ctx, f"w_r{j}", rw_h[j], E, 8)
        rbc = P.sb(f"rbc{j}", [8, 1]); P.dma(rbc[:], rb_h[j])
        P.dma(growb[0:1, 0:8], gb_h[j])
        P.mm(ctx.psb[3][0:128, 0:8], ctx.ones[0:1, :], growb[0:1, 0:8])
        P.vcopy(gbbc[:], ctx.psb[3][0:128, 0:8])
        for ci in range(NCH):
            ps = ctx.psb[0]
            for kt in range(KT8):
                P.mm(ps[0:8, 0:TC], wr[:, kt * 8:(kt + 1) * 8],
                     xfb[:, kt * NT + ci * TC: kt * NT + (ci + 1) * TC],
                     start=(kt == 0), stop=(kt == KT8 - 1))
            P.ts(lgT[:, 0:TC], ps[0:8, 0:TC], rbc[:, 0:1], None, ALU.add)
            for t4 in range(4):
                t = ci * 4 + t4
                P.trans(ctx.psb[3][0:128, 0:8], lgT[0:8, t4 * 128:(t4 + 1) * 128], ident[0:8, 0:8])
                P.vcopy(lg_tm[:], ctx.psb[3][0:128, 0:8])
                P.ts(lg_tm[:, 6:8], lg_tm[:, 6:8], 0.0, -1e30, ALU.mult, ALU.add)
                P.tt(bi_tm[:], lg_tm[:], gbbc[:], ALU.add)
                P.memset(sel_tm[:], 0.0)
                topk_iter2(P, ctx, bi_tm[:], sel_tm[:], 2)
                P.stt(bi_tm[:], sel_tm[:], 1e30, lg_tm[:], ALU.mult, ALU.add)
                P.ts(bi_tm[:], bi_tm[:], -1e30, None, ALU.add)
                P.red(ctx.cols[0][:, 0:1], bi_tm[:], ALU.max)
                P.ts(ctx.cols[0][:, 0:1], ctx.cols[0][:, 0:1], -1.0, None, ALU.mult)
                P.op("scalar", lambda e: e.activation(bi_tm[:], bi_tm[:], AF.Exp,
                                                      bias=ctx.cols[0][:, 0:1], scale=1.0))
                P.red(ctx.cols[1][:, 0:1], bi_tm[:], ALU.add)
                P.recip(ctx.cols[1][:, 0:1], ctx.cols[1][:, 0:1])
                P.ts(w_tm[:, t * 8:(t + 1) * 8], bi_tm[:], ctx.cols[1][:, 0:1],
                     None, ALU.mult)
        for e in range(6):
            wfill(P, ctx, wup, wup_h[e], E, E)
            wfill(P, ctx, wgg, wg_h[e], E, E)
            wfill(P, ctx, wdd, wd_h[e], E, E)
            P.dma(bupc[:], bup_h[e])
            P.dma(bgc[:], bg_h[e])
            P.dma(bdc[:], bd_h[e])
            for ci in range(NCH):
                for nt in range(KT8):
                    for kt in range(KT8):
                        P.mm(ctx.psb[0][:, 0:TC],
                             wup[:, kt * E + nt * 128: kt * E + nt * 128 + 128],
                             xfb[:, kt * NT + ci * TC: kt * NT + (ci + 1) * TC],
                             start=(kt == 0), stop=(kt == KT8 - 1))
                    P.ts(hb[:, nt * TC:(nt + 1) * TC], ctx.psb[0][:, 0:TC],
                         bupc[:, nt:nt + 1], None, ALU.add)
                for nt in range(KT8):
                    for kt in range(KT8):
                        P.mm(ctx.psb[0][:, 0:TC],
                             wgg[:, kt * E + nt * 128: kt * E + nt * 128 + 128],
                             hb[:, kt * TC:(kt + 1) * TC],
                             start=(kt == 0), stop=(kt == KT8 - 1))
                    P.op("scalar", lambda e2, nt=nt: e2.activation(
                        g2b[:, nt * TC:(nt + 1) * TC], ctx.psb[0][:, 0:TC], AF.Silu,
                        bias=bgc[:, nt:nt + 1], scale=1.0))
                P.tt(hb[:], g2b[:], hb[:], ALU.mult)
                bc = gate_row_bc(P, ctx, w_tm, 8, e, ci * TC, TC, ident,
                                 ctx.psb[3], growb)
                for dt in range(KT8):
                    for kt in range(KT8):
                        P.mm(ctx.psb[2][:, 0:TC],
                             wdd[:, kt * E + dt * 128: kt * E + dt * 128 + 128],
                             hb[:, kt * TC:(kt + 1) * TC],
                             start=(kt == 0), stop=(kt == KT8 - 1))
                    P.ts(ctx.tmps[0][:, 0:TC], ctx.psb[2][:, 0:TC],
                         bdc[:, dt:dt + 1], None, ALU.add)
                    P.tt(ctx.tmps[0][:, 0:TC], ctx.tmps[0][:, 0:TC], bc, ALU.mult)
                    a = acc[:, dt * NT + ci * TC: dt * NT + (ci + 1) * TC]
                    P.tt(a, a, ctx.tmps[0][:, 0:TC], ALU.add)
        for ns in range(2):
            wfill(P, ctx, wup, sup_h[:, ns * E:(ns + 1) * E], E, E)
            P.dma(bupc[:], sbup_h[ns])
            if ns == 0:
                wfill(P, ctx, wgg, swg_h, E, E)
                wfill(P, ctx, wdd, swd_h, E, E)
                P.dma(bgc[:], sbg_h[:])
                P.dma(bdc[:], sbd_h[:])
            for ci in range(NCH):
                for nt in range(KT8):
                    for kt in range(KT8):
                        P.mm(ctx.psb[0][:, 0:TC],
                             wup[:, kt * E + nt * 128: kt * E + nt * 128 + 128],
                             xfb[:, kt * NT + ci * TC: kt * NT + (ci + 1) * TC],
                             start=(kt == 0), stop=(kt == KT8 - 1))
                    P.ts(hb[:, nt * TC:(nt + 1) * TC], ctx.psb[0][:, 0:TC],
                         bupc[:, nt:nt + 1], None, ALU.add)
                for nt in range(KT8):
                    for kt in range(KT8):
                        P.mm(ctx.psb[0][:, 0:TC],
                             wgg[:, kt * E + nt * 128: kt * E + nt * 128 + 128],
                             hb[:, kt * TC:(kt + 1) * TC],
                             start=(kt == 0), stop=(kt == KT8 - 1))
                    P.op("scalar", lambda e2, nt=nt: e2.activation(
                        g2b[:, nt * TC:(nt + 1) * TC], ctx.psb[0][:, 0:TC], AF.Silu,
                        bias=bgc[:, nt:nt + 1], scale=1.0))
                P.tt(hb[:], g2b[:], hb[:], ALU.mult)
                for dt in range(KT8):
                    for kt in range(KT8):
                        P.mm(ctx.psb[2][:, 0:TC],
                             wdd[:, kt * E + dt * 128: kt * E + dt * 128 + 128],
                             hb[:, kt * TC:(kt + 1) * TC],
                             start=(kt == 0), stop=(kt == KT8 - 1))
                    P.ts(ctx.tmps[0][:, 0:TC], ctx.psb[2][:, 0:TC],
                         bdc[:, dt:dt + 1], None, ALU.add)
                    a = acc[:, dt * NT + ci * TC: dt * NT + (ci + 1) * TC]
                    P.tt(a, a, ctx.tmps[0][:, 0:TC], ALU.add)
        P.dma(nwc[:], nw_h[j])
        for ci in range(NCH):
            for ft in range(KT8):
                P.dma(ctx.stage[:, 0:TC], in_hbm[ft * 128:(ft + 1) * 128,
                                                 ci * TC:(ci + 1) * TC])
                a = acc[:, ft * NT + ci * TC: ft * NT + (ci + 1) * TC]
                P.tt(a, a, ctx.stage[:, 0:TC], ALU.add)

            def store_r(ft, ap, ci=ci):
                if rstore is not None:
                    P.dma(rstore[ft * 128:(ft + 1) * 128, ci * TC:(ci + 1) * TC], ap)

            rms_chunk(P, ctx, acc, NT, ci * TC, TC, 1e-6, store_r, gcol=nwc)

    one_round(0, x_h, r1_h)
    one_round(1, r1_h, None)
    residual_ln_store(P, ctx, acc, NT, x_h, out_h)
    return P.build()


def _cc(a):
    return np.ascontiguousarray(a, dtype=np.float32)


def _cols(b):
    n = b.shape[-1]
    return _cc(b.reshape(n // 128, 128).T)


def _consts():
    a = 1.0 / 10000.0 ** np.linspace(0.0, 1.0, 32)
    angle = np.repeat(a, 2)
    decay = np.log(1.0 - 2.0 ** (-5.0 - np.arange(16.0)))
    return angle, decay


def _retnet_tables(c):
    angle, decay = _consts()
    offb = (c % 4) * NT
    t = offb + np.arange(NT)
    cos2 = np.cos(t[None, :] * angle[:, None])
    sin2 = np.sin(t[None, :] * angle[:, None])
    cos2 = _cc(np.tile(cos2, (2, 1)))
    sin2 = _cc(np.tile(sin2, (2, 1)))
    f = np.arange(E)
    hh = f // 64
    ed = np.exp(decay)
    qpos = t
    e_d = decay[hh][:, None]
    s = (1.0 - np.exp(e_d * (qpos[None, :] + 1.0))) / (1.0 - np.exp(e_d))
    facQ = _cc(np.exp(e_d * (qpos[None, :] % TC)) / np.sqrt(s))
    j = np.arange(TC)
    facK = _cc(np.exp(-decay[hh][:, None] * j[None, :]))
    col = np.zeros((128, H * 64), np.float64)
    for h in range(H):
        for qc in range(NCH):
            q0 = offb + qc * TC
            for kc in range(NKC):
                k0 = kc * TC
                col[:, h * 64 + qc * 16 + kc] = np.exp(decay[h] * (q0 - k0)) if k0 <= q0 else 0.0
    sgn = np.where(np.arange(128) % 2 == 0, -1.0, 1.0).reshape(128, 1)
    return cos2, sin2, _cc(col), _cc(sgn)


def _qk_idx(c, causal):
    offb = (c % 4) * NT
    if causal:
        qidx = _cc((offb + np.arange(NT)).reshape(1, NT))
    else:
        qidx = _cc(np.full((1, NT), 1e9))
    kidx = np.zeros((128, NKC * 4), np.float32)
    for kc in range(NKC):
        for kt in range(4):
            kidx[:, kc * 4 + kt] = kc * TC + kt * 128 + np.arange(128)
    return qidx, _cc(kidx)


_PROGS = {}


def _prog(key, builder):
    if key not in _PROGS:
        _PROGS[key] = builder()
    return _PROGS[key]


def _launch(nc, in_maps):
    res = run_bass_kernel_spmd(nc, in_maps, core_ids=list(range(NCORES)))
    return res.results


def _gfm(res, key):
    return np.concatenate([np.asarray(res[c][key]) for c in range(NCORES)], axis=1)


def _gtm(res, key):
    return np.concatenate([np.asarray(res[c][key]) for c in range(NCORES)], axis=0)


def _slc(act, c):
    return _cc(act[:, c * NT:(c + 1) * NT])


def _fmb(act, c):
    b = c // 4
    return _cc(act[:, b * S:(b + 1) * S])


def _tmb(act, c):
    b = c // 4
    return _cc(act[b * S:(b + 1) * S, :])


IDENT = np.eye(128, dtype=np.float32)


def run_chain(inp, dump=None):
    x0 = np.asarray(inp["L0_attn_x"], np.float32)
    xf = _cc(x0.reshape(B * S, E).T)

    def dmp(name, val):
        if dump is not None:
            dump[name] = val

    perm = np.arange(E); perm = perm + np.where(perm % 2 == 0, 1, -1)

    nc = _prog("l0a_proj", b_l0a_proj)
    ims = []
    for c in range(NCORES):
        cos2, sin2, colfac, sgn = _retnet_tables(c)
        ims.append({"x": _slc(xf, c),
                    "Wq": _cc(inp["L0_attn_Wq"]), "Wk": _cc(inp["L0_attn_Wk"]),
                    "Wv": _cc(inp["L0_attn_Wv"]), "Wg": _cc(inp["L0_attn_Wg"]),
                    "WqP": _cc(np.asarray(inp["L0_attn_Wq"])[:, perm]),
                    "WkP": _cc(np.asarray(inp["L0_attn_Wk"])[:, perm]),
                    "cos2": cos2, "sin2": sin2, "sgncol": sgn})
    res = _launch(nc, ims)
    qr = _gfm(res, "qr"); kr = _gfm(res, "kr"); vt = _gtm(res, "v"); gg = _gfm(res, "g")
    dmp("l0_qr", qr)

    nc = _prog("l0a_attn", b_l0a_attn)
    ims = []
    for c in range(NCORES):
        cos2, sin2, colfac, sgn = _retnet_tables(c)
        angle, decay = _consts()
        offb = (c % 4) * NT
        tt = offb + np.arange(NT)
        f = np.arange(E); hh = f // 64
        e_d = decay[hh][:, None]
        s = (1.0 - np.exp(e_d * (tt[None, :] + 1.0))) / (1.0 - np.exp(e_d))
        facQ = _cc(np.exp(e_d * (tt[None, :] % TC)) / np.sqrt(s))
        facK = _cc(np.exp(-decay[hh][:, None] * np.arange(TC)[None, :]))
        qidx, kidx = _qk_idx(c, True)
        ims.append({"qr": _slc(qr, c), "krF": _fmb(kr, c), "vF": _tmb(vt, c),
                    "g": _slc(gg, c), "x": _slc(xf, c), "facQ": facQ, "facK": facK,
                    "colfac": colfac, "qidx": qidx, "kidx": kidx})
    res = _launch(nc, ims)
    x1 = _gfm(res, "xo")
    dmp("x1", x1)

    nc = _prog("l0f", b_l0f)
    G = np.zeros((32, 8), np.float32)
    for e in range(32):
        G[e, e // 4] = 1.0
    egT = _cc(np.asarray(inp["L0_ffn_expert_gate"]).transpose(0, 2, 1))
    euT = _cc(np.asarray(inp["L0_ffn_expert_up"]).transpose(0, 2, 1))
    edT = _cc(np.asarray(inp["L0_ffn_expert_down"]).transpose(0, 2, 1))
    ims = [{"x": _slc(x1, c), "gwT": _cc(np.asarray(inp["L0_ffn_gate_w"]).T),
            "egT": egT, "euT": euT, "edT": edT,
            "sgT": _cc(np.asarray(inp["L0_ffn_shared_gate_w"]).T),
            "suT": _cc(np.asarray(inp["L0_ffn_shared_up_w"]).T),
            "sdT": _cc(np.asarray(inp["L0_ffn_shared_down_w"]).T),
            "G": G, "G2": _cc(G.T), "ident": IDENT} for c in range(NCORES)]
    res = _launch(nc, ims)
    x2 = _gfm(res, "xo")
    dmp("x2", x2)

    nc = _prog("qkv_b", lambda: b_qkv_proj(True))
    ims = [{"x": _slc(x2, c), "Wq": _cc(np.asarray(inp["L1_attn_Wq"]).T),
            "Wk": _cc(np.asarray(inp["L1_attn_Wk"]).T),
            "Wv": _cc(np.asarray(inp["L1_attn_Wv"]).T),
            "bq": _cols(np.asarray(inp["L1_attn_bq"])),
            "bk": _cols(np.asarray(inp["L1_attn_bk"])),
            "bv": _cols(np.asarray(inp["L1_attn_bv"])),
            "bvrow": _cc(np.asarray(inp["L1_attn_bv"]).reshape(1, E))}
           for c in range(NCORES)]
    res = _launch(nc, ims)
    q1 = _gfm(res, "q"); k1 = _gfm(res, "k"); v1 = _gtm(res, "v")

    nc = _prog("soft_plain", lambda: b_soft_attn(False, "Wo", False))
    ims = []
    for c in range(NCORES):
        qidx, kidx = _qk_idx(c, True)
        ims.append({"q": _slc(q1, c), "kF": _fmb(k1, c), "vF": _tmb(v1, c),
                    "x": _slc(x2, c), "Wo": _cc(np.asarray(inp["L1_attn_Wo"]).T),
                    "bo": _cols(np.asarray(inp["L1_attn_bo"])),
                    "qidx": qidx, "kidx": kidx})
    res = _launch(nc, ims)
    x3 = _gfm(res, "xo")
    dmp("x3", x3)

    nc = _prog("qkv_nb", lambda: b_qkv_proj(False))
    ims = [{"x": _slc(x3, c), "Wq": _cc(inp["L1_ffn_Wq"]), "Wk": _cc(inp["L1_ffn_Wk"]),
            "Wv": _cc(inp["L1_ffn_Wv"])} for c in range(NCORES)]
    res = _launch(nc, ims)
    qf = _gfm(res, "q"); kf = _gfm(res, "k"); vf = _gtm(res, "v")

    nc = _prog("soft_ln", lambda: b_soft_attn(False, "Wo", True))
    ims = []
    for c in range(NCORES):
        qidx, kidx = _qk_idx(c, False)
        ims.append({"q": _slc(qf, c), "kF": _fmb(kf, c), "vF": _tmb(vf, c),
                    "x": _slc(x3, c), "Wo": _cc(inp["L1_ffn_Wo"]),
                    "bo": _cols(np.asarray(inp["L1_ffn_bo"])),
                    "lng": _cols(np.asarray(inp["L1_ffn_ln1_g"])),
                    "lnb": _cols(np.asarray(inp["L1_ffn_ln1_b"])),
                    "qidx": qidx, "kidx": kidx})
    res = _launch(nc, ims)
    out1 = _gfm(res, "xo")
    dmp("out1", out1)

    nc = _prog("l1f_moe", b_l1f_moe)
    be1r = np.stack([_cols(np.asarray(inp["L1_ffn_be1"])[e]) for e in range(8)])
    be2r = np.stack([_cols(np.asarray(inp["L1_ffn_be2"])[e]) for e in range(8)])
    ims = [{"out1": _slc(out1, c), "x3": _slc(x3, c),
            "Wr": np.pad(_cc(inp["L1_ffn_Wr"]), ((0, 0), (0, 0))),
            "We1": _cc(inp["L1_ffn_We1"]), "be1r": _cc(be1r),
            "We2": _cc(inp["L1_ffn_We2"]), "be2r": _cc(be2r),
            "ln2g": _cols(np.asarray(inp["L1_ffn_ln2_g"])),
            "ln2b": _cols(np.asarray(inp["L1_ffn_ln2_b"])),
            "ident": IDENT} for c in range(NCORES)]
    res = _launch(nc, ims)
    x4 = _gfm(res, "xo")
    dmp("x4", x4)

    nc = _prog("qkv_nb", lambda: b_qkv_proj(False))
    ims = [{"x": _slc(x4, c), "Wq": _cc(np.asarray(inp["L2_attn_Wq"]).reshape(E, E)),
            "Wk": _cc(np.asarray(inp["L2_attn_Wk"]).reshape(E, E)),
            "Wv": _cc(np.asarray(inp["L2_attn_Wv"]).reshape(E, E))}
           for c in range(NCORES)]
    res = _launch(nc, ims)
    q2 = _gfm(res, "q"); k2 = _gfm(res, "k"); v2 = _gtm(res, "v")

    nc = _prog("soft_plain", lambda: b_soft_attn(False, "Wo", False))
    ims = []
    for c in range(NCORES):
        qidx, kidx = _qk_idx(c, True)
        ims.append({"q": _slc(q2, c), "kF": _fmb(k2, c), "vF": _tmb(v2, c),
                    "x": _slc(x4, c),
                    "Wo": _cc(np.asarray(inp["L2_attn_Wout"]).reshape(E, E)),
                    "bo": np.zeros((128, KT8), np.float32),
                    "qidx": qidx, "kidx": kidx})
    res = _launch(nc, ims)
    x5 = _gfm(res, "xo")
    dmp("x5", x5)

    nc = _prog("l2f_a", b_l2f_a)
    ims = [{"x": _slc(x5, c), "lng": _cols(np.asarray(inp["L2_ffn_ln_gamma"])),
            "lnb": _cols(np.asarray(inp["L2_ffn_ln_beta"])),
            "fc1q": np.ascontiguousarray(inp["L2_ffn_fc1_q"], np.int32),
            "fc1s": _cc(inp["L2_ffn_fc1_scales"]),
            "b1r": _cols(np.asarray(inp["L2_ffn_fc1_bias"]))} for c in range(NCORES)]
    res = _launch(nc, ims)
    ys = [np.asarray(res[c]["y"]) for c in range(NCORES)]

    nc = _prog("l2f_b", b_l2f_b)
    ims = [{"x": _slc(x5, c), "y": ys[c],
            "fc2q": np.ascontiguousarray(inp["L2_ffn_fc2_q"], np.int32),
            "fc2s": _cc(inp["L2_ffn_fc2_scales"]),
            "b2r": _cols(np.asarray(inp["L2_ffn_fc2_bias"]))} for c in range(NCORES)]
    res = _launch(nc, ims)
    x6 = _gfm(res, "xo")
    dmp("x6", x6)

    nc = _prog("l3a_proj", b_l3a_proj)
    ims = [{"x": _slc(x6, c), "wqkv": _cc(inp["L3_attn_w_qkv"])} for c in range(NCORES)]
    res = _launch(nc, ims)
    q3 = _gfm(res, "q"); k3 = _gfm(res, "k"); v3 = _gtm(res, "v")
    qlp = [np.asarray(res[c]["qlp"]) for c in range(NCORES)]
    klp = [np.asarray(res[c]["klp"]) for c in range(NCORES)]
    Qls = [np.concatenate(qlp[4 * b:4 * b + 4], axis=1) for b in range(2)]
    Kls = [np.concatenate(klp[4 * b:4 * b + 4], axis=1) for b in range(2)]

    nc = _prog("l3a_core", b_l3a_core)
    ims = []
    for c in range(NCORES):
        b = c // 4
        ims.append({"q": _slc(q3, c), "kF": _fmb(k3, c), "vF": _tmb(v3, c),
                    "Ql": _cc(Qls[b]), "Kl": _cc(Kls[b]), "x": _slc(x6, c),
                    "wproj": _cc(inp["L3_attn_w_proj"]), "ident": IDENT,
                    "eye7": _cc(7 * np.eye(64)), "eye15": _cc(15 * np.eye(64)),
                    "eye13": _cc(13 * np.eye(64))})
    res = _launch(nc, ims)
    x7 = _gfm(res, "xo")
    dmp("x7", x7)

    nc = _prog("l3f", b_l3f)
    rW = np.zeros((2, E, 8), np.float32); rW[:, :, :6] = np.asarray(inp["L3_ffn_router_W"])
    rb = np.zeros((2, 8, 1), np.float32); rb[:, :6, 0] = np.asarray(inp["L3_ffn_router_b"])
    gb = np.zeros((2, 1, 8), np.float32); gb[:, 0, :6] = np.asarray(inp["L3_ffn_gate_bias"])
    ebupr = np.stack([_cols(np.asarray(inp["L3_ffn_exp_bup"])[e]) for e in range(6)])
    ebgr = np.stack([_cols(np.asarray(inp["L3_ffn_exp_bg"])[e]) for e in range(6)])
    ebdr = np.stack([_cols(np.asarray(inp["L3_ffn_exp_bdown"])[e]) for e in range(6)])
    sbupr = np.stack([_cols(np.asarray(inp["L3_ffn_share_bup"])[ns * 1024:(ns + 1) * 1024])
                      for ns in range(2)])
    nwr = np.stack([_cols(np.asarray(inp["L3_ffn_norm_w"])[j]) for j in range(2)])
    ims = [{"x": _slc(x7, c), "rW": rW, "rb": rb, "gbias": gb,
            "eWup": _cc(inp["L3_ffn_exp_Wup"]), "ebupr": _cc(ebupr),
            "eWg": _cc(inp["L3_ffn_exp_Wg"]), "ebgr": _cc(ebgr),
            "eWd": _cc(inp["L3_ffn_exp_Wdown"]), "ebdr": _cc(ebdr),
            "sWup": _cc(inp["L3_ffn_share_Wup"]), "sbupr": _cc(sbupr),
            "sWg": _cc(inp["L3_ffn_share_Wg"]),
            "sbgr": _cols(np.asarray(inp["L3_ffn_share_bg"])),
            "sWd": _cc(inp["L3_ffn_share_Wdown"]),
            "sbdr": _cols(np.asarray(inp["L3_ffn_share_bdown"])),
            "normw": _cc(nwr), "ident": IDENT} for c in range(NCORES)]
    res = _launch(nc, ims)
    xo = _gfm(res, "xo")

    out = np.empty((B, S, E), np.float32)
    for c in range(NCORES):
        b, off = c // 4, (c % 4) * NT
        out[b, off:off + NT, :] = np.asarray(res[c]["xo"]).T
    return out


def kernel(**inputs):
    return run_chain(inputs)
